# version 24
# speedup vs baseline: 1.1244x; 1.0807x over previous
import numpy as np
import concourse.bass as bass
import concourse.mybir as mybir
from concourse.bass_utils import run_bass_kernel_spmd
from contextlib import ExitStack

F32 = mybir.dt.float32
BF16 = mybir.dt.bfloat16
AF = mybir.ActivationFunctionType
ALU = mybir.AluOpType
AX = mybir.AxisListType
I32 = mybir.dt.int32

D = 1024
NCH = D // 128
H, DK = 16, 64
E, FF = 8, 4096
BS = 64
LN_EPS = 1e-5

PHASES = 4
STAGE = "full"
STAGE_B = "full"
SPARSE_FFN = True
EMIT_ONLY = None
DBG_G2 = False


NOSYNC = True
NS_ACT = False


class MK:

    ENGS = ("t", "s", "v", "p", "q")

    def __init__(self, nc, es):
        self.nc = nc
        self._es = es
        self.items = []
        self.cnt = {}
        self.wcache = {e: {} for e in self.ENGS}
        self.last_op = {e: None for e in self.ENGS}
        self.sems = {}
        self.dma_sems = {"q": set(), "p": set()}
        for name in ("vS", "sS", "pS", "tS", "dQ", "dP", "bar"):
            self.sems[name] = es.enter_context(nc.semaphore(name))
            self.cnt[name] = 0

    def wait(self, eng, sem, val):
        if val <= 0 or self.wcache[eng].get(sem, 0) >= val:
            return
        self.wcache[eng][sem] = val
        self.items.append(["wait", eng, sem, val])

    def op(self, eng, fn, waits=(), inc=None):
        for w in waits:
            if w is None:
                continue
            if w[0] == "dma":
                self.wait(eng, w[1], self.cnt[w[1]])
            else:
                self.wait(eng, w[0], w[1])
        it = ["op", eng, fn, [inc] if inc else []]
        self.items.append(it)
        self.last_op[eng] = it
        if inc:
            self.cnt[inc[0]] += inc[1]

    def v(self, fn, waits=(), ns=False):
        if not (ns and NOSYNC):
            self.wait("v", "vS", self.cnt["vS"])
        self.op("v", fn, waits, inc=("vS", 1))
        return ("vS", self.cnt["vS"])

    def s(self, fn, waits=(), ns=False):
        if not (ns and NOSYNC and NS_ACT):
            self.wait("s", "sS", self.cnt["sS"])
        self.op("s", fn, waits, inc=("sS", 1))
        return ("sS", self.cnt["sS"])

    def p(self, fn, waits=(), ns=False):
        if not (ns and NOSYNC):
            self.wait("p", "pS", self.cnt["pS"])
        self.op("p", fn, waits, inc=("pS", 1))
        return ("pS", self.cnt["pS"])

    def t(self, fn, waits=(), mark=False):
        self.op("t", fn, waits, inc=(("tS", 1) if mark else None))
        return ("tS", self.cnt["tS"]) if mark else None

    def dma(self, eng, fn, waits=(), sem=None):
        if sem is None:
            sem = "dQ" if eng == "q" else "dP"
            self.op(eng, fn, waits, inc=(sem, 16))
            return ("dma", sem)
        if sem not in self.cnt:
            self.sems[sem] = self._es.enter_context(self.nc.semaphore(sem))
            self.cnt[sem] = 0
        self.dma_sems.setdefault(eng, set()).add(sem)
        self.wait(eng, sem, self.cnt[sem])
        self.op(eng, fn, waits, inc=(sem, 16))
        return (sem, self.cnt[sem])

    def tok(self, sem):
        return (sem, self.cnt[sem])

    def barrier(self):
        target = self.cnt["bar"] + len(self.ENGS)
        for e in self.ENGS:
            it = self.last_op[e]
            if it is not None and not it[3]:
                it[3].append(("bar", 1))
            else:
                if e == "v":
                    self.wait(e, "vS", self.cnt["vS"])
                elif e == "s":
                    self.wait(e, "sS", self.cnt["sS"])
                elif e == "p":
                    self.wait(e, "pS", self.cnt["pS"])
                    self.wait(e, "dP", self.cnt["dP"])
                    for ds in sorted(self.dma_sems.get("p", ())):
                        self.wait(e, ds, self.cnt[ds])
                elif e == "q":
                    self.wait(e, "dQ", self.cnt["dQ"])
                    for ds in sorted(self.dma_sems.get("q", ())):
                        self.wait(e, ds, self.cnt[ds])
                elif it is not None:
                    self.wait(e, it[3][0][0], self.cnt[it[3][0][0]])
                sem = self.sems["bar"]
                self.op(e, lambda en, s=sem: en.sem_inc(s, 1))
            self.cnt["bar"] += 1
        for e in self.ENGS:
            self.wait(e, "bar", target)
            self.last_op[e] = None

    def replay(self, block):
        sems = self.sems

        def body(engobj, ename):
            for it in self.items:
                if it[1] != ename:
                    continue
                if it[0] == "wait":
                    engobj.wait_ge(sems[it[2]], it[3])
                else:
                    ins = it[2](engobj)
                    for (sem, amt) in it[3]:
                        ins = ins.then_inc(sems[sem], amt)

        block.tensor(lambda e: body(e, "t"))
        block.scalar(lambda e: body(e, "s"))
        block.vector(lambda e: body(e, "v"))
        block.gpsimd(lambda e: body(e, "p"))
        block.sync(lambda e: body(e, "q"))


def build(n_tok, g_blocks, phases=PHASES):
    NS = n_tok // 128
    NT = n_tok // 512
    NB = n_tok // BS
    nc = bass.Bass("TRN2", target_bir_lowering=False)

    def dt_in(name, shape):
        return nc.dram_tensor(name, shape, F32, kind="ExternalInput").ap()

    x_in = dt_in("x", [n_tok, D])
    aWQ = dt_in("aWQ", [D, D]); aWK = dt_in("aWK", [D, D])
    aWV = dt_in("aWV", [D, D]); aWfc = dt_in("aWfc", [D, D])
    gamma = dt_in("gamma", [D]); beta = dt_in("beta", [D])
    bWqT = dt_in("bWqT", [D, D]); bWkT = dt_in("bWkT", [D, D])
    bWvT = dt_in("bWvT", [D, D]); bWoT = dt_in("bWoT", [D, D])
    bqr = dt_in("bqr", [128, NCH]); bkr = dt_in("bkr", [128, NCH])
    bv = dt_in("bv", [D]); bo = dt_in("bo", [D])
    gate_w = dt_in("gate_w", [D, E])
    w1 = dt_in("w1", [E, D, FF]); b1t = dt_in("b1t", [E, 128, FF // 128])
    w2 = dt_in("w2", [E, FF, D]); b2 = dt_in("b2", [E, D])
    out = nc.dram_tensor("out", [n_tok, D], F32, kind="ExternalOutput").ap()

    CCAP = 384 * max(1, round(n_tok * 3 / 8 / 384))
    w1b_d = nc.dram_tensor("w1b_d", [E, D, FF], BF16,
                           kind="Internal").ap()
    w2b_d = nc.dram_tensor("w2b_d", [E, FF, D], BF16,
                           kind="Internal").ap()
    wcast = {"w1": [None] * E, "w2": [None] * E}
    usort_d = nc.dram_tensor("usort_d", [E * CCAP + 128, D], BF16,
                             kind="Internal").ap()
    ysort_f32 = nc.dram_tensor("ysort_f32", [E * CCAP + 128, D], F32,
                               kind="Internal").ap()
    cnts_d = nc.dram_tensor("cnts_d", [1, 8 * (n_tok // 128)], F32,
                            kind="Internal").ap()
    offs_d = nc.dram_tensor("offs_d", [1, 8 * (n_tok // 128)], F32,
                            kind="Internal").ap()
    qT_d = nc.dram_tensor("qT_d", [D, n_tok], BF16, kind="Internal").ap()
    kT_d = nc.dram_tensor("kT_d", [D, n_tok], BF16, kind="Internal").ap()
    v_d = nc.dram_tensor("v_d", [n_tok, D], BF16, kind="Internal").ap()
    ctx_d = nc.dram_tensor("ctx_d", [n_tok, D], BF16, kind="Internal").ap()

    es = ExitStack()
    with es:
        sb = lambda name, shape, dt: es.enter_context(nc.sbuf_tensor(name, shape, dt))
        mk = MK(nc, es)

        x_sb = sb("x_sb", [128, NS * D], F32)
        stats = sb("stats", [128, NS * 2], F32)
        stmp = sb("stmp", [128, 4], F32)
        ident = sb("ident", [128, 128], BF16)
        ones_f32r = sb("ones_f32r", [1, 128], F32)
        ones_bfr = sb("ones_bfr", [1, 128], BF16)
        onescol_bf = sb("onescol_bf", [128, 1], BF16)
        epsA = sb("epsA", [128, 1], F32)
        TRIU = sb("TRIU", [128, 128], BF16)
        iota8 = sb("iota8", [128, 8], F32)
        trash_c = sb("trash_c", [128, 1], F32)
        p0_es = ExitStack()
        iota_f = p0_es.enter_context(nc.sbuf_tensor("iota_f", [128, 128], F32))

        def xs(s):
            return x_sb[:, s * D:(s + 1) * D]

        def r_ap(s):
            return stats[:, 2 * s:2 * s + 1]

        def nmr_ap(s):
            return stats[:, 2 * s + 1:2 * s + 2]

        tok_x = mk.dma("q", lambda q: q.dma_start(
            x_sb[:].rearrange("p (s d) -> p s d", d=D),
            x_in.rearrange("(s p) d -> p s d", p=128)))

        t_io = mk.p(lambda g: g.iota(iota_f[:], [[1, 128]], channel_multiplier=-1,
                                     allow_small_or_imprecise_dtypes=True))
        mk.p(lambda g: g.iota(iota8[:], [[1, 8]], channel_multiplier=0,
                              allow_small_or_imprecise_dtypes=True))
        mk.p(lambda g: g.memset(onescol_bf[:], 1.0))
        mk.p(lambda g: g.memset(trash_c[:], float(E * CCAP)))
        mk.v(lambda v: v.tensor_scalar(TRIU[:], iota_f[:], 0.0, None,
                                       ALU.is_gt), waits=[("pS", 1)])
        zrow = p0_es.enter_context(nc.sbuf_tensor("zrow", [128, D], BF16))
        zrow_f = p0_es.enter_context(nc.sbuf_tensor("zrow_f", [128, D], F32))
        mk.p(lambda g: g.memset(zrow[:], 0.0))
        mk.p(lambda g: g.memset(zrow_f[:], 0.0))
        t_zr = mk.tok("pS")
        for r0 in range(0, E * CCAP + 128, 128):
            mk.dma("q", lambda q, r_=r0: q.dma_start(
                usort_d[r_:r_ + 128, :], zrow[:]), waits=[t_zr])
            mk.dma("q", lambda q, r_=r0: q.dma_start(
                ysort_f32[r_:r_ + 128, :], zrow_f[:]), waits=[t_zr])

        mk.p(lambda g: g.memset(ones_f32r[:], 1.0))
        mk.p(lambda g: g.memset(ones_bfr[:], 1.0))
        t_ones = mk.p(lambda g: g.memset(epsA[:], LN_EPS))
        t_ident = mk.v(lambda v: v.tensor_scalar(ident[:], iota_f[:], 0.0, None,
                                                 ALU.is_equal), waits=[t_io])

        mk.barrier()
        p0_es.close()

        sqh = [None]

        def stats_maker(dep):
            toks = []

            def emit_for(s):
                sm = stmp[:, (s % 2) * 2:(s % 2) * 2 + 1]
                sq = stmp[:, (s % 2) * 2 + 1:(s % 2) * 2 + 2]
                r = r_ap(s)
                nmr = nmr_ap(s)
                t_sq = mk.s(lambda sc, s_=s, q_=sq, o_=sqh[0]: sc.activation(
                    o_[:], xs(s_), AF.Square, accum_out=q_),
                    waits=list(dep) + ([toks[s - 2]] if s >= 2 else []))
                mk.v(lambda v, s_=s, m_=sm: v.reduce_sum(m_, xs(s_), axis=AX.X),
                     waits=dep)
                mk.v(lambda v, m_=sm: v.tensor_scalar(m_, m_, 1.0 / D, None,
                                                      ALU.mult))
                mk.v(lambda v, q_=sq: v.tensor_scalar(q_, q_, 1.0 / D, None,
                                                      ALU.mult), waits=[t_sq])
                mk.v(lambda v, m_=sm, r_=r: v.tensor_tensor(r_, m_, m_,
                                                            ALU.mult))
                mk.v(lambda v, q_=sq, r_=r: v.tensor_tensor(q_, q_, r_,
                                                            ALU.subtract))
                t_var = mk.tok("vS")
                t_std = mk.s(lambda sc, q_=sq: sc.activation(
                    q_, q_, AF.Sqrt, bias=epsA[:], scale=1.0), waits=[t_var])
                mk.v(lambda v, r_=r, q_=sq: v.reciprocal(r_, q_),
                     waits=[t_std])
                toks.append(mk.v(lambda v, m_=sm, r_=r, n_=nmr:
                                 v.scalar_tensor_tensor(n_, m_, -1.0, r_,
                                                        ALU.mult, ALU.mult)))
            return emit_for, toks

        def stats_sweep(dep):
            emit_for, toks = stats_maker(dep)
            for s in range(NS):
                emit_for(s)
            return toks

        def emit_u(s, dest, waits=()):
            return mk.s(lambda sc: sc.activation(dest, xs(s), AF.Identity,
                                                 bias=nmr_ap(s), scale=r_ap(s)),
                        waits=list(waits))

        def emit_transpose8(src_tile, tp_bank, dest_view, waits_pe=(),
                            waits_dve=(), evac="v"):
            tm = None
            for c in range(NCH):
                tm = mk.t(lambda t, c_=c: t.matmul(
                    tp_bank[:, c_ * 128:(c_ + 1) * 128],
                    src_tile[:, c_ * 128:(c_ + 1) * 128], ident[:],
                    start=(c_ == 0), stop=(c_ == NCH - 1), is_transpose=True),
                    waits=(list(waits_pe) + [t_ident]) if c == 0 else [],
                    mark=(c == NCH - 1))
            if evac == "s":
                return mk.s(lambda sc: sc.copy(
                    dest_view,
                    tp_bank[:].rearrange("p (c t) -> p c t", t=128)),
                    waits=[tm] + list(waits_dve))
            return mk.v(lambda v: v.tensor_copy(
                dest_view, tp_bank[:].rearrange("p (c t) -> p c t", t=128)),
                waits=[tm] + list(waits_dve))

        def emit_attn_a():
            if STAGE == "p0":
                return
            aw_es = ExitStack()
            awsb = lambda n_, sh, dt: aw_es.enter_context(
                nc.sbuf_tensor(n_, sh, dt))
            sqh[0] = awsb("sqa", [128, D], BF16)
            st = stats_sweep([tok_x])
            if STAGE == "stats":
                mk.barrier()
                return
            gB = awsb("gB", [128, D], F32)
            bB = awsb("bB", [128, D], F32)
            row_a = awsb("row_a", [1, D], F32)
            with nc.psum_tensor("bc_ps", [128, 1024], F32) as bc_ps:
                tm_prev = None
                ev_prev = None
                for vec, dest in ((gamma, gB), (beta, bB)):
                    tokr = mk.dma("q", lambda q, vv=vec: q.dma_start(
                        row_a[:], vv[None, :]), waits=[tm_prev], sem="dRowA")
                    tm = None
                    for hh2 in range(2):
                        tm = mk.t(lambda t, h2=hh2: t.matmul(
                            bc_ps[:, h2 * 512:(h2 + 1) * 512], ones_f32r[:],
                            row_a[:, h2 * 512:(h2 + 1) * 512],
                            start=True, stop=True),
                            waits=[tokr, t_ones, ev_prev], mark=(hh2 == 1))
                    ev_prev = mk.v(lambda v, dd=dest: v.tensor_copy(
                        dd[:], bc_ps[:]), waits=[tm])
                    tm_prev = tm
            gb_dve = ev_prev
            wk_sb = awsb("wk_sb", [128, NCH * D], BF16)
            wv_sb = awsb("wv_sb", [128, NCH * D], BF16)
            u_bf = awsb("u_bf", [128, 2 * D], BF16)
            KVk = awsb("KVk", [128, H * 66], BF16)
            a1_es = ExitStack()
            a1sb = lambda n_, sh, dt: a1_es.enter_context(
                nc.sbuf_tensor(n_, sh, dt))
            uT_bf = a1sb("uT_bf", [128, 2 * D], BF16)
            K_sb = a1sb("K_sb", [128, 2 * D], BF16)
            V_sb = a1sb("V_sb", [128, 2 * (H * 65)], BF16)

            def wload(dst, src):
                return mk.dma("p", lambda g: g.dma_start(
                    dst[:].rearrange("p (c j) -> p c j", j=D),
                    src.rearrange("(c p) j -> p c j", p=128)))

            tok_wk = wload(wk_sb, aWK)
            tok_wv = wload(wv_sb, aWV)
            t_vone = mk.p(lambda g: g.memset(
                V_sb[:].rearrange("p (b h c) -> p (b h) c", b=2, c=65)[:, :, 64:65],
                1.0))
            mk.p(lambda g: g.memset(KVk[:], 0.0))
            tok_kvz = mk.tok("pS")

            with ExitStack() as ps:
                pst = lambda name, shape, dt: ps.enter_context(
                    nc.psum_tensor(name, shape, dt))
                tp = pst("tp_a1", [128, NCH * 128], BF16)
                kp = [pst(f"kp{i}", [128, 512], F32) for i in range(2)]
                vp = [pst(f"vp{i}", [128, 512], F32) for i in range(2)]
                kvp = [pst(f"kvp{i}", [64, 512], F32) for i in range(3)]

                uT_tok = [None, None]
                kse = [None, None]
                vse = [None, None]
                kevac = [None, None]
                vevac = [None, None]
                kvdone = [None, None]

                def emit_kv(s):
                    b = s % 2
                    last = None
                    for h in range(H):
                        bank, off = (h // 7, (h % 7) * 66) if h < 14 else (2, (h - 14) * 66)
                        first_in_bank = h in (0, 7, 14)
                        last_in_bank = h in (6, 13, 15)
                        last = mk.t(lambda t, h_=h, b_=b, bk=bank, o=off,
                                    st_=(s == 0 and first_in_bank),
                                    sp_=(s == NS - 1 and last_in_bank): t.matmul(
                            kvp[bk][:, o:o + 65],
                            K_sb[:, b_ * D + h_ * 64: b_ * D + (h_ + 1) * 64],
                            V_sb[:, b_ * (H * 65) + h_ * 65: b_ * (H * 65) + h_ * 65 + 65],
                            start=st_, stop=sp_),
                            waits=[kse[b], vse[b], tok_kvz] if h == 0 else [],
                            mark=(h == H - 1))
                    kvdone[b] = last
                    return last

                for s in range(NS):
                    b = s % 2
                    t_u = emit_u(s, u_bf[:, b * D:(b + 1) * D],
                                 waits=[st[s], uT_tok[b]])
                    uT_tok[b] = emit_transpose8(
                        u_bf[:, b * D:(b + 1) * D], tp,
                        uT_bf[:, b * D:(b + 1) * D].rearrange(
                            "p (c t) -> p c t", t=128),
                        waits_pe=[t_u, uT_tok[1 - b], gb_dve])
                    if s > 0:
                        emit_kv(s - 1)
                    for W_sb, pp, wtok, evl in ((wk_sb, kp, tok_wk, kevac),
                                                (wv_sb, vp, tok_wv, vevac)):
                        for half in range(2):
                            for c in range(NCH):
                                mk.t(lambda t, c_=c, h_=half, W=W_sb, P=pp, b_=b: t.matmul(
                                    P[h_][:],
                                    uT_bf[:, b_ * D + c_ * 128: b_ * D + (c_ + 1) * 128],
                                    W[:, c_ * D + h_ * 512: c_ * D + h_ * 512 + 512],
                                    start=(c_ == 0), stop=(c_ == NCH - 1)),
                                    waits=([uT_tok[b], wtok, evl[half]]
                                           if c == 0 else []),
                                    mark=(c == NCH - 1))
                    tmm = mk.tok("tS")
                    for half in range(2):
                        kevac[half] = mk.s(lambda sc, h_=half, b_=b: sc.activation(
                            K_sb[:, b_ * D + h_ * 512: b_ * D + h_ * 512 + 512],
                            kp[h_][:], AF.Sigmoid), waits=[tmm, kvdone[b]])
                        vevac[half] = mk.v(lambda v, h_=half, b_=b: v.tensor_copy(
                            V_sb[:, b_ * (H * 65):(b_ + 1) * (H * 65)].rearrange(
                                "p (h c) -> p h c", c=65)[:, h_ * 8:(h_ + 1) * 8, 0:64],
                            vp[h_][:].rearrange("p (h j) -> p h j", j=64)),
                            waits=[tmm, kvdone[b]])
                    kse[b] = kevac[1]
                    vse[b] = vevac[1]
                emit_kv(NS - 1)
                t_kv = mk.tok("tS")
                kvk_tok = None
                for i in range(3):
                    nh = 7 if i < 2 else 2
                    for pb in (0, 64):
                        kvk_tok = mk.v(lambda v, i_=i, n_=nh, p_=pb: v.tensor_copy(
                            KVk[p_:p_ + 64, i_ * 7 * 66:(i_ * 7 + n_) * 66].rearrange(
                                "p (h c) -> p h c", c=66)[:, :, 0:65],
                            kvp[i_][0:64, 0:n_ * 66].rearrange(
                                "p (h c) -> p h c", c=66)[:, :, 0:65]),
                            waits=[t_kv])
            mk.barrier()
            a1_es.close()
            if STAGE == "pass1":
                return

            wq_sb, wfc_sb = wk_sb, wv_sb
            tok_wq = wload(wq_sb, aWQ)
            tok_wfc = wload(wfc_sb, aWfc)
            if SPARSE_FFN and phases >= 2:
                for e in range(E):
                    wcast["w1"][e] = mk.dma("p", lambda g, e_=e: g.dma_start(
                        w1b_d[e_], w1[e_]), sem="dWc")
                    wcast["w2"][e] = mk.dma("p", lambda g, e_=e: g.dma_start(
                        w2b_d[e_], w2[e_]), sem="dWc")
            a2_es = ExitStack()
            a2sb = lambda n_, sh, dt: a2_es.enter_context(
                nc.sbuf_tensor(n_, sh, dt))
            uT2 = a2sb("uT2_a", [128, NCH * 512], BF16)
            qT_sb = a2sb("qT_a", [128, NCH * 512], BF16)
            G_sb = a2sb("G_a", [128, 2 * D], BF16)
            GT_sb = a2sb("GT_a", [128, D], BF16)
            den_sb = a2sb("den_a", [128, H], F32)
            ln2_tmp = a2sb("ln2_tmp", [128, 4], F32)
            z_sb = a2sb("z_a", [128, D], F32)

            with ExitStack() as ps:
                pst = lambda name, shape, dt: ps.enter_context(
                    nc.psum_tensor(name, shape, dt))
                tp2 = [pst(f"tp2_{i}", [128, NCH * 128], BF16) for i in range(2)]
                qtp = [pst(f"qtp{i}", [128, 512], F32) for i in range(2)]
                hp = [pst(f"hp{i}", [128, 512], F32) for i in range(2)]
                op = [pst(f"op{i}", [128, 512], F32) for i in range(2)]

                HGRP = [[0, 2, 4, 6, 8, 10, 12],
                        [1, 3, 5, 7, 9, 11, 13], [14], [15]]
                wfc_done = [None]
                qse = [None, None]
                hge = [None, None]
                ope = [None, None]
                uT2_tok = None
                tp2_evac = [None, None]

                def emit_heads(T, su, pend):
                    sg = T * 4 + su
                    gtoks = []
                    GBASE = [0, 7, 14, 15]
                    for gi, grp in enumerate(HGRP):
                        bnk = gi % 2
                        gb = GBASE[gi]
                        tm = None
                        for ii, h in enumerate(grp):
                            tm = mk.t(lambda t, h_=h, o=ii * 66, b_=bnk,
                                      st_=(ii == 0),
                                      sp_=(ii == len(grp) - 1): t.matmul(
                                hp[b_][:, o:o + 66],
                                qT_sb[(h_ % 2) * 64:(h_ % 2) * 64 + 64,
                                      (h_ // 2) * 512 + su * 128:
                                      (h_ // 2) * 512 + su * 128 + 128],
                                KVk[(h_ % 2) * 64:(h_ % 2) * 64 + 64,
                                    h_ * 66:(h_ + 1) * 66],
                                start=st_, stop=sp_),
                                waits=([qse[0], qse[1], hge[bnk], kvk_tok, pend]
                                       if ii == 0 else []),
                                mark=(ii == len(grp) - 1))
                        ng = len(grp)
                        mk.v(lambda v, g0=gb, n_=ng, b_=bnk: v.tensor_scalar(
                            bass.AP(den_sb.ap().tensor, g0,
                                    [[H, 128], [1, n_]]),
                            bass.AP(hp[b_].ap().tensor, 64,
                                    [[512, 128], [66, n_]]),
                            1e-8, None, ALU.add), waits=[tm])
                        mk.v(lambda v, g0=gb, n_=ng: v.reciprocal(
                            den_sb[:, g0:g0 + n_], den_sb[:, g0:g0 + n_]))
                        gtoks.append(mk.v(
                            lambda v, g0=gb, n_=ng, b_=bnk, h0=grp[0]:
                            v.tensor_tensor(
                                bass.AP(G_sb.ap().tensor,
                                        (su % 2) * D + h0 * 64,
                                        [[2 * D, 128], [128, n_], [1, 64]]),
                                bass.AP(hp[b_].ap().tensor, 0,
                                        [[512, 128], [66, n_], [1, 64]]),
                                bass.AP(den_sb.ap().tensor, g0,
                                        [[H, 128], [1, n_], [0, 64]]),
                                ALU.mult)))
                        hge[bnk] = gtoks[-1]
                    return gtoks[-1]

                def emit_tail(T, su, g_tok):
                    s = T * 4 + su
                    b = su % 2
                    tp2_evac[b] = emit_transpose8(
                        G_sb[:, b * D:(b + 1) * D], tp2[b],
                        GT_sb[:].rearrange("p (c t) -> p c t", t=128),
                        waits_pe=[g_tok, tp2_evac[b]],
                        waits_dve=[wfc_done[0]], evac="s")
                    tm = None
                    for half in range(2):
                        for c in range(NCH):
                            tm = mk.t(lambda t, c_=c, h_=half: t.matmul(
                                op[h_][:],
                                GT_sb[:, c_ * 128:(c_ + 1) * 128],
                                wfc_sb[:, c_ * D + h_ * 512: c_ * D + h_ * 512 + 512],
                                start=(c_ == 0), stop=(c_ == NCH - 1)),
                                waits=([tp2_evac[b], tok_wfc, ope[half]]
                                       if c == 0 else []),
                                mark=(c == NCH - 1 and half == 1))
                    wfc_done[0] = tm
                    for half in range(2):
                        mk.v(lambda v, h_=half: v.scalar_tensor_tensor(
                            z_sb[:, h_ * 512:(h_ + 1) * 512],
                            xs(s)[:, h_ * 512:(h_ + 1) * 512], r_ap(s),
                            op[h_][:], ALU.mult, ALU.add), waits=[tm])
                    ope[0] = ope[1] = tok_z = mk.tok("vS")
                    c0 = ln2_tmp[:, 0:1]
                    c1 = ln2_tmp[:, 1:2]
                    c2 = ln2_tmp[:, 2:3]
                    t_sq = mk.s(lambda sc, o_=sqh[0]: sc.activation(
                        o_[:], z_sb[:], AF.Square, accum_out=c1),
                        waits=[tok_z])
                    mk.v(lambda v: v.reduce_sum(c0, z_sb[:], axis=AX.X))
                    mk.v(lambda v: v.tensor_scalar(c0, c0, 1.0 / D, None,
                                                   ALU.mult))
                    mk.v(lambda v: v.tensor_scalar(c1, c1, 1.0 / D, None,
                                                   ALU.mult), waits=[t_sq])
                    mk.v(lambda v: v.tensor_tensor(c2, c0, c0, ALU.mult))
                    t_var = mk.v(lambda v: v.tensor_tensor(c1, c1, c2,
                                                           ALU.subtract))
                    t_std = mk.s(lambda sc: sc.activation(
                        c1, c1, AF.Sqrt, bias=epsA[:], scale=1.0),
                        waits=[t_var])
                    mk.v(lambda v: v.reciprocal(c1, c1), waits=[t_std])
                    t_c2 = mk.v(lambda v: v.scalar_tensor_tensor(
                        c2, c0, -1.0, c1, ALU.mult, ALU.mult))
                    t_n = mk.s(lambda sc: sc.activation(
                        z_sb[:], z_sb[:], AF.Identity, bias=c2, scale=c1),
                        waits=[t_c2])
                    mk.v(lambda v: v.tensor_tensor(z_sb[:], z_sb[:], gB[:],
                                                   ALU.mult), waits=[t_n])
                    mk.v(lambda v: v.tensor_tensor(z_sb[:], z_sb[:], bB[:],
                                                   ALU.add))
                    return mk.v(lambda v: v.tensor_tensor(xs(s), xs(s), z_sb[:],
                                                          ALU.add))

                for T in range(NT):
                    for su in range(4):
                        b = su % 2
                        t_u = emit_u(T * 4 + su, u_bf[:, b * D:(b + 1) * D],
                                     waits=[uT_tok[b]])
                        uT_tok[b] = tp2_evac[b] = emit_transpose8(
                            u_bf[:, b * D:(b + 1) * D], tp2[b],
                            uT2[:].rearrange("p (c t) -> p c t", t=512)
                            [:, :, su * 128:su * 128 + 128],
                            waits_pe=[t_u, tp2_evac[b]])
                    uT2_tok = uT_tok[1]
                    for c8 in range(NCH):
                        bq_ = c8 % 2
                        tm = None
                        for c in range(NCH):
                            tm = mk.t(lambda t, c_=c, c8_=c8, b_=bq_: t.matmul(
                                qtp[b_][:],
                                wq_sb[:, c_ * D + c8_ * 128: c_ * D + c8_ * 128 + 128],
                                uT2[:, c_ * 512:(c_ + 1) * 512],
                                start=(c_ == 0), stop=(c_ == NCH - 1)),
                                waits=([uT2_tok, tok_wq, qse[bq_]]
                                       if c == 0 else []),
                                mark=(c == NCH - 1))
                        qse[bq_] = mk.s(lambda sc, c8_=c8, b_=bq_: sc.activation(
                            qT_sb[:, c8_ * 512:(c8_ + 1) * 512], qtp[b_][:],
                            AF.Sigmoid), waits=[tm])
                    pend = None
                    gtk = emit_heads(T, 0, None)
                    for su in range(1, 4):
                        gtk_next = emit_heads(T, su, None)
                        emit_tail(T, su - 1, gtk)
                        gtk = gtk_next
                    emit_tail(T, 3, gtk)
            mk.barrier()
            a2_es.close()
            aw_es.close()

        TPG = min(2, NT)
        NFCB = 4

        ffn_call_idx = [0]

        def emit_ffn():
            ci = ffn_call_idx[0]
            ffn_call_idx[0] += 1
            st = stats_sweep([])
            f_es = ExitStack()
            fsb = lambda n_, sh, dt: f_es.enter_context(
                nc.sbuf_tensor(n_ + "_%d" % ci, sh, dt))
            gate_sb = fsb("gate_sb", [128, NCH * 8], BF16)
            b1t_sb = fsb("b1t_sb", [128, E * 32], F32)
            b2_sb = fsb("b2_sb", [8, D], BF16)
            tok_gate = mk.dma("p", lambda g: g.dma_start(
                gate_sb[:].rearrange("p (c e) -> p c e", e=8),
                gate_w.rearrange("(c p) e -> p c e", p=128)),
                sem="dCg%d" % ci)
            tok_b1t = mk.dma("q", lambda q: q.dma_start(
                b1t_sb[:].rearrange("p (e c) -> p e c", c=32),
                b1t.rearrange("e p c -> p e c")), sem="dCb1%d" % ci)
            tok_b2 = mk.dma("p", lambda g: g.dma_start(b2_sb[:], b2),
                            sem="dCb2%d" % ci)
            u4 = fsb("u4", [128, 2 * D], BF16)
            uT4 = fsb("uT4", [128, NCH * TPG * 512], BF16)
            H4 = fsb("H4", [128, 8 * 512], BF16)
            wb = fsb("wb", [128, NCH * D], BF16)
            wc = fsb("wc", [128, 8 * D], BF16)
            lg4 = fsb("lg4", [128, TPG * 32], F32)
            cb4 = fsb("cb4", [128, TPG * 32], F32)
            cb_bf = fsb("cb_bf", [128, 8], BF16)
            cT4 = fsb("cT4", [8, TPG * 512], BF16)
            gt4 = fsb("gt4", [128, 16], F32)

            def w1_load(e, fcB, waits):
                return mk.dma("p", lambda g: g.dma_start(
                    wb[:].rearrange("p (c f) -> p c f", f=D),
                    w1[e, :, fcB * D:(fcB + 1) * D].rearrange(
                        "(c p) f -> p c f", p=128)), waits=waits, sem="dW1")

            def w2_load(e, fcB, waits):
                return mk.dma("p", lambda g: g.dma_start(
                    wc[:].rearrange("p (c d) -> p c d", d=D),
                    w2[e, fcB * D:(fcB + 1) * D, :].rearrange(
                        "(c p) d -> p c d", p=128)), waits=waits, sem="dW2")

            NGRP = NT // TPG
            SEQ = [(e, fb) for g_ in range(NGRP)
                   for e in range(E) for fb in range(NFCB)]
            uT_tok2 = [None, None]
            tpe = [None, None]
            w1tok = w1_load(0, 0, list(sc_toks))
            w2tok = w2_load(0, 0, list(sc_toks))
            ybfree = [None] * 8
            hev = [None] * 8

            for grp in range(NT // TPG):
                tiles = list(range(grp * TPG, (grp + 1) * TPG))
                pre_es = ExitStack()
                pp = lambda n_, sh, dt: pre_es.enter_context(
                    nc.psum_tensor(n_, sh, dt))
                tp2 = [pp(f"ftp{ci}_{grp}_{i}", [128, NCH * 128], BF16)
                       for i in range(2)]
                lgp = pp(f"lgp{ci}_{grp}", [128, 512], F32)
                ctp = pp(f"ctp{ci}_{grp}", [128, 1024], BF16)
                b2p = [pp(f"b2p{ci}_{grp}_{i}", [128, 512], F32)
                       for i in range(2)]

                lg_last = None
                for tl, T in enumerate(tiles):
                    for su in range(4):
                        b = su % 2
                        t_u = emit_u(T * 4 + su, u4[:, b * D:(b + 1) * D],
                                     waits=[st[T * 4 + su], uT_tok2[b]])
                        uT_tok2[b] = tpe[b] = emit_transpose8(
                            u4[:, b * D:(b + 1) * D], tp2[b],
                            uT4[:].rearrange("p (c t) -> p c t", t=TPG * 512)
                            [:, :, tl * 512 + su * 128: tl * 512 + su * 128 + 128],
                            waits_pe=[t_u, tpe[b]])
                uTgrp = mk.tok("vS")
                nlg = 0
                for tl, T in enumerate(tiles):
                    for su in range(4):
                        for c in range(NCH):
                            lg_last = mk.t(lambda t, c_=c, o=(tl * 4 + su) * 8,
                                           tc=tl * 512 + su * 128,
                                           st_=(nlg == 0 and c == 0),
                                           sp_=(nlg == TPG * 4 - 1
                                                and c == NCH - 1): t.matmul(
                                lgp[:, o:o + 8],
                                uT4[:, c_ * TPG * 512 + tc:
                                     c_ * TPG * 512 + tc + 128],
                                gate_sb[:, c_ * 8:(c_ + 1) * 8],
                                start=st_, stop=sp_),
                                waits=[uTgrp, tok_gate] if (nlg == 0 and c == 0)
                                else [], mark=(c == NCH - 1))
                        nlg += 1
                t_lg = mk.v(lambda v: v.tensor_copy(lg4[:], lgp[:, 0:TPG * 32]),
                            waits=[lg_last])
                cmb_toks = []
                ct_last = None
                nct = 0
                for tl, T in enumerate(tiles):
                    for su in range(4):
                        o = (tl * 4 + su) * 8
                        L = lg4[:, o:o + 8]
                        CB = cb4[:, o:o + 8]
                        m1 = gt4[:, 0:1]; m2 = gt4[:, 1:2]
                        ee = gt4[:, 2:3]; s1 = gt4[:, 3:4]; s2 = gt4[:, 4:5]
                        scr = gt4[:, 8:16]
                        mk.v(lambda v, L=L, m1=m1: v.reduce_max(m1, L, axis=AX.X),
                             waits=[t_lg])
                        mk.v(lambda v, L=L, CB=CB, m1=m1: v.tensor_scalar(
                            CB, L, m1, -1e30, ALU.is_equal, ALU.mult))
                        mk.v(lambda v, L=L, CB=CB: v.tensor_tensor(
                            CB, CB, L, ALU.add))
                        mk.v(lambda v, CB=CB, m2=m2: v.reduce_max(m2, CB, axis=AX.X))
                        t_d = mk.v(lambda v, m1=m1, m2=m2, ee=ee: v.tensor_tensor(
                            ee, m2, m1, ALU.subtract))
                        t_e = mk.s(lambda sc, ee=ee: sc.activation(
                            ee, ee, AF.Exp), waits=[t_d])
                        mk.v(lambda v, ee=ee, s1=s1: v.tensor_scalar(
                            s1, ee, 1.0, None, ALU.add), waits=[t_e])
                        mk.v(lambda v, s1=s1: v.reciprocal(s1, s1))
                        mk.v(lambda v, ee=ee, s1=s1, s2=s2: v.tensor_tensor(
                            s2, ee, s1, ALU.mult))
                        mk.v(lambda v, CB=CB, m2=m2, s2=s2, scr=scr:
                             v.tensor_scalar(scr, CB, m2, s2, ALU.is_equal,
                                             ALU.mult))
                        mk.v(lambda v, L=L, CB=CB, m1=m1, s1=s1: v.tensor_scalar(
                            CB, L, m1, s1, ALU.is_equal, ALU.mult))
                        t_cb = mk.v(lambda v, CB=CB, scr=scr: v.tensor_tensor(
                            CB, CB, scr, ALU.add))
                        cmb_toks.append(t_cb)
                        t_cbf = mk.v(lambda v, CB=CB: v.tensor_copy(cb_bf[:], CB),
                                     waits=[ct_last])
                        ct_last = mk.t(lambda t, o2=(tl * 4 + su) * 128,
                                       st_=(nct == 0),
                                       sp_=(nct == TPG * 4 - 1): t.matmul(
                            ctp[0:8, o2:o2 + 128], cb_bf[:], ident[:],
                            start=st_, stop=sp_, is_transpose=True),
                            waits=[t_cbf], mark=True)
                        nct += 1
                t_ct = mk.v(lambda v: v.tensor_copy(cT4[0:8, :],
                                                    ctp[0:8, 0:TPG * 512]),
                            waits=[ct_last])
                b2e = [None, None]
                for tl, T in enumerate(tiles):
                    for su in range(4):
                        s_ = T * 4 + su
                        bb_ = su % 2
                        for half in range(2):
                            tmb = mk.t(lambda t, o=(tl * 4 + su) * 128, h_=half,
                                       b_=bb_: t.matmul(
                                b2p[b_][:], cT4[0:8, o:o + 128],
                                b2_sb[0:8, h_ * 512:(h_ + 1) * 512],
                                start=True, stop=True),
                                waits=[t_ct, tok_b2, b2e[bb_]], mark=True)
                            b2e[bb_] = mk.v(lambda v, s__=s_, h_=half, b_=bb_:
                                            v.scalar_tensor_tensor(
                                xs(s__)[:, h_ * 512:(h_ + 1) * 512],
                                b2p[b_][:], 1.0,
                                xs(s__)[:, h_ * 512:(h_ + 1) * 512],
                                ALU.mult, ALU.add), waits=[tmb])
                pre_v = mk.tok("vS")
                pre_s = mk.tok("sS")
                pre_es.close()

                y_es = ExitStack()
                Y = [y_es.enter_context(nc.psum_tensor(f"Y{ci}_{grp}_{k}",
                                                       [128, 512], F32))
                     for k in range(8)]
                for e in range(E):
                    for fcB in range(NFCB):
                        for tl, T in enumerate(tiles):
                            for fc in range(8):
                                for c in range(NCH):
                                    mk.t(lambda t, fc_=fc, c_=c, tl_=tl: t.matmul(
                                        Y[fc_ % 2][:],
                                        wb[:, c_ * D + fc_ * 128:
                                           c_ * D + fc_ * 128 + 128],
                                        uT4[:, c_ * TPG * 512 + tl_ * 512:
                                             c_ * TPG * 512 + tl_ * 512 + 512],
                                        start=(c_ == 0), stop=(c_ == NCH - 1)),
                                        waits=([w1tok, uTgrp, ybfree[fc % 2],
                                                pre_v, pre_s,
                                                hev[fc - 2] if fc >= 2 else None]
                                               if c == 0 else []),
                                        mark=(c == NCH - 1))
                                tmfc = mk.tok("tS")
                                hev[fc] = mk.s(lambda sc, fc_=fc, e_=e, fb=fcB:
                                               sc.activation(
                                    H4[:, fc_ * 512:(fc_ + 1) * 512],
                                    Y[fc_ % 2][:], AF.Gelu,
                                    bias=b1t_sb[:, e_ * 32 + fb * 8 + fc_:
                                                e_ * 32 + fb * 8 + fc_ + 1]),
                                    waits=[tmfc, tok_b1t])
                            idx = grp * E * NFCB + e * NFCB + fcB
                            if tl == len(tiles) - 1 and idx + 1 < len(SEQ):
                                ne, nf = SEQ[idx + 1]
                                w1tok = w1_load(ne, nf, [mk.tok("tS")])
                            for fc in range(8):
                                for su in range(4):
                                    for half in range(2):
                                        k = su * 2 + half
                                        wl = []
                                        if su == 0 and half == 0:
                                            wl += [w2tok, hev[fc]]
                                        if fc == 0:
                                            wl += [ybfree[k], pre_v, pre_s]
                                            if k < 2:
                                                wl.append(hev[6 + k])
                                        mk.t(lambda t, fc_=fc, su_=su, h_=half,
                                             k_=k: t.matmul(
                                            Y[k_][:],
                                            H4[:, fc_ * 512 + su_ * 128:
                                               fc_ * 512 + su_ * 128 + 128],
                                            wc[:, fc_ * D + h_ * 512:
                                               fc_ * D + h_ * 512 + 512],
                                            start=(fc_ == 0), stop=(fc_ == 7)),
                                            waits=wl,
                                            mark=(fc == 7 and k == 7))
                            tmy = mk.tok("tS")
                            if tl == len(tiles) - 1 and idx + 1 < len(SEQ):
                                ne, nf = SEQ[idx + 1]
                                w2tok = w2_load(ne, nf, [tmy])
                            for su in range(4):
                                for half in range(2):
                                    k = su * 2 + half
                                    s_ = T * 4 + su
                                    ybfree[k] = mk.v(
                                        lambda v, s__=s_, h_=half, k_=k,
                                        o=(tl * 4 + su) * 8 + e:
                                        v.scalar_tensor_tensor(
                                            xs(s__)[:, h_ * 512:(h_ + 1) * 512],
                                            Y[k_][:], cb4[:, o:o + 1],
                                            xs(s__)[:, h_ * 512:(h_ + 1) * 512],
                                            ALU.mult, ALU.add), waits=[tmy])
                y_es.close()
                mk.barrier()
            f_es.close()

        CT = 384
        NTC = CCAP // CT

        def emit_ffn_sparse():
            ci = ffn_call_idx[0]
            ffn_call_idx[0] += 1
            f_es = ExitStack()
            fsb = lambda n_, sh, dt: f_es.enter_context(
                nc.sbuf_tensor(n_ + "_s%d" % ci, sh, dt))
            sig_sb = fsb("sig", [128, 2 * NS], F32)
            e12_sb = fsb("e12", [128, 2 * NS], F32)
            idxq = fsb("idxq", [128, 2 * NS], I32)
            g_es = ExitStack()
            gsb = lambda n_, sh, dt: g_es.enter_context(
                nc.sbuf_tensor(n_ + "_g%d" % ci, sh, dt))
            sqh[0] = gsb("sqf", [128, D], BF16)
            st_emit, st = stats_maker([])
            for s in range(min(4, NS)):
                st_emit(s)
            rk_sb = gsb("rk", [128, 8 * NS], F32)
            u4 = gsb("u4", [128, 2 * D], BF16)
            uT_s = gsb("uTs", [128, 2 * D], BF16)
            L8 = gsb("L8", [128, 64], F32)
            sA = gsb("sA", [128, 64], F32)
            sB = gsb("sB", [128, 64], F32)
            sC = gsb("sC", [128, 64], F32)
            gc = gsb("gc", [128, 48], F32)
            ind_bf = gsb("indbf", [128, 64], BF16)
            cnt_row = gsb("cntrow", [1, 8 * NS], F32)
            cnt32 = gsb("cnt32", [NS, 8], BF16)
            offs32 = gsb("offs32", [NS, 8], F32)
            offsall = gsb("offsall", [1, 8 * NS], F32)
            d12 = gsb("d12", [128, 16], F32)
            gate_sb = gsb("gate", [128, NCH * 8], BF16)
            tok_gate = mk.dma("p", lambda g: g.dma_start(
                gate_sb[:].rearrange("p (c e) -> p c e", e=8),
                gate_w.rearrange("(c p) e -> p c e", p=128)),
                sem="dCg%d" % ci)

            with ExitStack() as ps:
                pst = lambda n_, sh, dt: ps.enter_context(
                    nc.psum_tensor(n_ + "_g%d" % ci, sh, dt))
                tpg2 = [pst(f"tpg{i}", [128, NCH * 128], BF16) for i in range(2)]
                lgp = [pst(f"lgs{i}", [128, 32], F32) for i in range(2)]
                rkp = [pst(f"rkp{i}", [128, 32], F32) for i in range(2)]
                cntp = pst("cntp", [1, 8 * NS], F32)
                ofbp = pst("ofbp", [128, 32], F32)

                uts_tok = [None, None]
                lg_ev = [None, None]
                rk_ev = [None, None]
                cnt_last = None
                NQ = (NS + 3) // 4

                def g1_stage1(q_):
                    for s2 in range(4 * (q_ + 1), min(4 * (q_ + 2), NS)):
                        st_emit(s2)
                    B = q_ % 2
                    subs = [x for x in range(4 * q_, min(4 * q_ + 4, NS))]
                    nj = len(subs)
                    tm = None
                    evs = [None] * nj

                    def logits(j):
                        nonlocal tm
                        b = j % 2
                        for c in range(NCH):
                            first = (j == 0 and c == 0)
                            tm = mk.t(lambda t, c_=c, j_=j, b_=b, B_=B:
                                      t.matmul(
                                lgp[B_][:, 8 * j_:8 * j_ + 8],
                                uT_s[:, b_ * D + c_ * 128:
                                     b_ * D + (c_ + 1) * 128],
                                gate_sb[:, c_ * 8:(c_ + 1) * 8],
                                start=(j_ == 0 and c_ == 0),
                                stop=(j_ == nj - 1 and c_ == NCH - 1)),
                                waits=(([tok_gate, lg_ev[B]] if first else [])
                                       + ([evs[j]] if c == 0 else [])),
                                mark=(j == nj - 1 and c == NCH - 1))

                    for j, s_ in enumerate(subs):
                        b = j % 2
                        t_u = emit_u(s_, u4[:, b * D:(b + 1) * D],
                                     waits=[st[s_], uts_tok[b]])
                        if j >= 2:
                            logits(j - 2)
                        uts_tok[b] = evs[j] = emit_transpose8(
                            u4[:, b * D:(b + 1) * D], tpg2[b],
                            uT_s[:, b * D:(b + 1) * D].rearrange(
                                "p (c t) -> p c t", t=128),
                            waits_pe=[t_u, uts_tok[b]])
                    for j in range(max(0, nj - 2), nj):
                        logits(j)
                    return tm

                def g1_stage2(q_, tm):
                    nonlocal cnt_last
                    B = q_ % 2
                    s0 = 4 * q_
                    nj = min(4, NS - s0)
                    nc8 = nj * 8
                    L = L8[:, B * 32:B * 32 + nc8]
                    A = sA[:, B * 32:B * 32 + nc8]
                    Bv = sB[:, B * 32:B * 32 + nc8]
                    C = sC[:, B * 32:B * 32 + nc8]
                    ib = ind_bf[:, B * 32:B * 32 + nc8]
                    gb_ = B * 24
                    m1 = gc[:, gb_:gb_ + nj]
                    m2 = gc[:, gb_ + 4:gb_ + 4 + nj]
                    g2 = gc[:, gb_ + 8:gb_ + 8 + nj]
                    seg = lambda t_, off: bass.AP(
                        t_.ap().tensor, off, [[64, 128], [8, nj], [1, 8]])
                    L3 = seg(L8, B * 32)
                    A3 = seg(sA, B * 32)
                    B3 = seg(sB, B * 32)
                    bc = lambda t_, off, w: bass.AP(
                        t_.ap().tensor, off, [[w, 128], [1, nj], [0, 8]])
                    m1b = bc(gc, gb_, 48)
                    m2b = bc(gc, gb_ + 4, 48)
                    iob = bass.AP(iota8.ap().tensor, 0,
                                  [[8, 128], [0, nj], [1, 8]])
                    lg_ev[B] = mk.s(lambda sc, B_=B, L_=L: sc.copy(
                        L_, lgp[B_][:, 0:nc8]), waits=[tm])
                    mk.v(lambda v, o=m1: v.reduce_max(o, L3, axis=AX.X),
                         waits=[lg_ev[B]])
                    mk.v(lambda v, L_=L, A_=A: v.tensor_tensor(
                        A_, L_, m1b, ALU.is_equal))
                    mk.v(lambda v, L_=L, A_=A: v.scalar_tensor_tensor(
                        A_, A_, -1e30, L_, ALU.mult, ALU.add))
                    mk.v(lambda v, o=m2: v.reduce_max(o, A3, axis=AX.X))
                    mk.v(lambda v, L_=L, B_=Bv: v.tensor_tensor(
                        B_, L_, m1b, ALU.is_equal))
                    mk.v(lambda v, A_=A: v.tensor_tensor(
                        A_, A_, m2b, ALU.is_equal))
                    mk.v(lambda v, A_=A, B_=Bv, C_=C: v.tensor_tensor(
                        C_, B_, A_, ALU.add))
                    t_sC = mk.tok("vS")
                    t_ind = mk.s(lambda sc, C_=C, i_=ib: sc.copy(i_, C_),
                                 waits=[t_sC])
                    mk.v(lambda v, B_=Bv: v.tensor_tensor(B_, B_, iob,
                                                          ALU.mult))
                    mk.v(lambda v: v.reduce_sum(
                        bass.AP(e12_sb.ap().tensor, 2 * s0,
                                [[2 * NS, 128], [2, nj]]), B3, axis=AX.X))
                    mk.v(lambda v, A_=A: v.tensor_tensor(A_, A_, iob,
                                                         ALU.mult))
                    mk.v(lambda v: v.reduce_sum(
                        bass.AP(e12_sb.ap().tensor, 2 * s0 + 1,
                                [[2 * NS, 128], [2, nj]]), A3, axis=AX.X))
                    mk.v(lambda v, o=g2, a=m2, b_=m1: v.tensor_tensor(
                        o, a, b_, ALU.subtract))
                    t_d = mk.tok("vS")
                    t_e = mk.s(lambda sc, o=g2: sc.activation(
                        o, o, AF.Exp), waits=[t_d])
                    sg1 = bass.AP(gc.ap().tensor, gb_ + 12,
                                  [[48, 128], [2, nj]])
                    sg2 = bass.AP(gc.ap().tensor, gb_ + 13,
                                  [[48, 128], [2, nj]])
                    mk.v(lambda v, i_=g2: v.tensor_scalar(
                        sg1, i_, 1.0, None, ALU.add), waits=[t_e])
                    mk.v(lambda v: v.reciprocal(sg1, sg1))
                    t_s2 = mk.v(lambda v, a=g2: v.tensor_tensor(
                        sg2, a, sg1, ALU.mult))
                    mk.s(lambda sc: sc.copy(
                        sig_sb[:, 2 * s0:2 * s0 + 2 * nj],
                        gc[:, gb_ + 12:gb_ + 12 + 2 * nj]),
                        waits=[t_s2])
                    t_rk = mk.t(lambda t, B_=B, i_=ib: t.matmul(
                        rkp[B_][:, 0:nc8], TRIU[:], i_, start=True, stop=True),
                        waits=[t_ind, rk_ev[B]], mark=True)
                    rk_ev[B] = mk.v(lambda v, B_=B: v.tensor_copy(
                        rk_sb[:, 8 * s0:8 * s0 + nc8], rkp[B_][:, 0:nc8]),
                        waits=[t_rk])
                    cnt_last = mk.t(lambda t, q__=q_, i_=ib: t.matmul(
                        cntp[0:1, 8 * s0:8 * s0 + nc8], onescol_bf[:],
                        i_, start=(q__ == 0), stop=(q__ == NQ - 1)),
                        waits=[t_ind], mark=True)

                tm_q = None
                for q_ in range(NQ):
                    tm_new = g1_stage1(q_)
                    if q_ >= 1:
                        g1_stage2(q_ - 1, tm_q)
                    tm_q = tm_new
                g1_stage2(NQ - 1, tm_q)
                t_cr = mk.v(lambda v: v.tensor_copy(cnt_row[:], cntp[:]),
                            waits=[cnt_last])
                t_cd = mk.dma("q", lambda q: q.dma_start(cnts_d, cnt_row[:]),
                              waits=[t_cr], sem="dCn%d" % ci)
                t_cl = mk.dma("p", lambda g: g.dma_start(
                    cnt32[:], cnts_d.rearrange("o (s e) -> (o s) e", e=8)),
                    waits=[t_cd], sem="dCl%d" % ci)
                t_of = mk.t(lambda t: t.matmul(
                    ofbp[0:NS, 0:8], TRIU[0:NS, 0:NS], cnt32[0:NS, :],
                    start=True, stop=True), waits=[t_cl], mark=True)
                t_ov = mk.v(lambda v: v.tensor_copy(offs32[:], ofbp[0:NS, 0:8]),
                            waits=[t_of])
                t_od = mk.dma("q", lambda q: q.dma_start(
                    offs_d.rearrange("o (s e) -> (o s) e", e=8), offs32[:]),
                    waits=[t_ov], sem="dOd%d" % ci)
                t_oc = mk.dma("q", lambda q: q.dma_start(
                    offsall[:], offs_d), waits=[t_od], sem="dOl%d" % ci)
                sc_toks = []
                u_sc = [[], []]
                prev_rk2 = t_ov
                ixq_tok = [None, None]

                def g2_idx(q_):
                    nonlocal prev_rk2
                    s0 = 4 * q_
                    nj = min(4, NS - s0)
                    nc8 = nj * 8
                    t_bc = mk.t(lambda t: t.matmul(
                        ofbp[:, 0:nc8], ones_f32r[:],
                        offsall[0:1, 8 * s0:8 * s0 + nc8],
                        start=True, stop=True),
                        waits=[t_oc, prev_rk2], mark=True)
                    A = sA[:, 0:nc8]
                    Bv = sB[:, 0:nc8]
                    prev_rk2 = mk.v(lambda v, A_=A: v.tensor_tensor(
                        A_, rk_sb[:, 8 * s0:8 * s0 + nc8], ofbp[:, 0:nc8],
                        ALU.add), waits=[t_bc])
                    B3g = bass.AP(sB.ap().tensor, 0, [[64, 128], [8, nj],
                                                      [1, 8]])
                    iob = bass.AP(iota8.ap().tensor, 0,
                                  [[8, 128], [0, nj], [1, 8]])
                    trb = bass.AP(trash_c.ap().tensor, 0, [[1, 128], [0, nj]])
                    for col in (0, 1):
                        ebc3 = bass.AP(e12_sb.ap().tensor, 2 * s0 + col,
                                       [[2 * NS, 128], [2, nj], [0, 8]])
                        est2 = bass.AP(e12_sb.ap().tensor, 2 * s0 + col,
                                       [[2 * NS, 128], [2, nj]])
                        d1 = d12[:, 0:nj]
                        dst = d12[:, 4:4 + nj]
                        ov = d12[:, 8:8 + nj]
                        tp_ = d12[:, 12:12 + nj]
                        mk.v(lambda v, B_=Bv, e_=ebc3: v.tensor_tensor(
                            B_, iob, e_, ALU.is_equal))
                        mk.v(lambda v, A_=A, B_=Bv: v.tensor_tensor(
                            B_, B_, A_, ALU.mult))
                        mk.v(lambda v, o=d1: v.reduce_sum(o, B3g, axis=AX.X))
                        mk.v(lambda v, o=dst, e_=est2: v.tensor_scalar(
                            o, e_, float(CCAP), None, ALU.mult))
                        mk.v(lambda v, o=dst, a=d1: v.tensor_tensor(
                            o, o, a, ALU.add))
                        mk.v(lambda v, o=ov, a=d1: v.tensor_scalar(
                            o, a, float(CCAP), None, ALU.is_ge))
                        mk.v(lambda v, o=tp_, a=dst: v.scalar_tensor_tensor(
                            o, a, -1.0, trb, ALU.mult, ALU.add))
                        mk.v(lambda v, o=tp_, b_=ov: v.tensor_tensor(
                            o, o, b_, ALU.mult))
                        mk.v(lambda v, o=dst, a=tp_: v.tensor_tensor(
                            o, o, a, ALU.add))
                        mk.v(lambda v, c_=col: v.tensor_copy(
                            bass.AP(idxq.ap().tensor, 2 * s0 + c_,
                                    [[2 * NS, 128], [2, nj]]),
                            d12[:, 4:4 + nj]))
                    ixq_tok[q_ % 2] = mk.tok("vS")

                def g2_scat(q_):
                    s0 = 4 * q_
                    for s_ in range(s0, min(s0 + 4, NS)):
                        b = s_ % 2
                        t_u = emit_u(s_, u4[:, b * D:(b + 1) * D],
                                     waits=list(u_sc[b]) or [uts_tok[b]])
                        u_sc[b] = []
                        for col in (0, 1):
                            u_sc[b].append(mk.dma(
                                "p", lambda g, s__=s_, b_=b, c_=col:
                                g.indirect_dma_start(
                                    usort_d, bass.IndirectOffsetOnAxis(
                                        idxq[:, 2 * s__ + c_:
                                             2 * s__ + c_ + 1], 0),
                                    u4[:, b_ * D:(b_ + 1) * D], None),
                                waits=[t_u, ixq_tok[q_ % 2]]
                                + ([u_sc[b][0]] if col == 1 else []),
                                sem="dSc%d_%d" % (ci, col)))
                            sc_toks.append(u_sc[b][-1])

                for q_ in range(NQ):
                    g2_idx(q_)
                    if q_ >= 1:
                        g2_scat(q_ - 1)
                g2_scat(NQ - 1)
                if DBG_G2 and ci == 0:
                    dbg = nc.dram_tensor(
                        "dbg", [128, 28 * NS + 48], F32,
                        kind="ExternalOutput").ap()
                    dbq = nc.dram_tensor(
                        "dbq", [128, 2 * NS], I32,
                        kind="ExternalOutput").ap()
                    mk.barrier()
                    mk.dma("q", lambda q: q.dma_start(
                        dbg[:, 0:2 * NS], sig_sb[:]), sem="dDb")
                    mk.dma("q", lambda q: q.dma_start(
                        dbg[:, 2 * NS:4 * NS], e12_sb[:]), sem="dDb")
                    mk.dma("q", lambda q: q.dma_start(
                        dbg[:, 4 * NS:12 * NS], rk_sb[:]), sem="dDb")
                    mk.dma("q", lambda q: q.dma_start(
                        dbg[0:1, 12 * NS:20 * NS], cnt_row[0:1, :]),
                        sem="dDb")
                    mk.dma("q", lambda q: q.dma_start(
                        dbg[0:1, 20 * NS:28 * NS], offsall[0:1, :]),
                        sem="dDb")
                    mk.dma("q", lambda q: q.dma_start(
                        dbq[:], idxq[:]), sem="dDb")
                    mk.dma("q", lambda q: q.dma_start(
                        dbg[:, 28 * NS + 8:28 * NS + 40], sA[:, 0:32]),
                        sem="dDb")
                    mk.v(lambda v: v.tensor_copy(
                        d12[:, 0:4],
                        bass.AP(e12_sb.ap().tensor, 0,
                                [[2 * NS, 128], [2, 4]])))
                    mk.v(lambda v: v.tensor_scalar(
                        d12[:, 4:8],
                        bass.AP(e12_sb.ap().tensor, 0,
                                [[2 * NS, 128], [2, 4]]),
                        1.0, None, ALU.mult))
                    tdd = mk.tok("vS")
                    mk.dma("q", lambda q: q.dma_start(
                        dbg[:, 28 * NS:28 * NS + 8], d12[:, 0:8]),
                        waits=[tdd], sem="dDb")
                    mk.barrier()
                    t_b2 = mk.t(lambda t: t.matmul(
                        rkp[0][:], ones_f32r[:], offsall[0:1, 0:32],
                        start=True, stop=True), mark=True)
                    mk.v(lambda v: v.tensor_tensor(
                        sA[:, 0:32], rk_sb[:, 0:32], rkp[0][:],
                        ALU.add), waits=[t_b2])
                    iob_ = bass.AP(iota8.ap().tensor, 0,
                                   [[8, 128], [0, 4], [1, 8]])
                    ebc3_ = bass.AP(e12_sb.ap().tensor, 0,
                                    [[2 * NS, 128], [2, 4], [0, 8]])
                    est2_ = bass.AP(e12_sb.ap().tensor, 0,
                                    [[2 * NS, 128], [2, 4]])
                    B3g_ = bass.AP(sB.ap().tensor, 0, [[64, 128], [8, 4],
                                                       [1, 8]])
                    mk.v(lambda v: v.tensor_tensor(
                        sB[:, 0:32], iob_, ebc3_, ALU.is_equal))
                    mk.v(lambda v: v.tensor_tensor(
                        sB[:, 0:32], sB[:, 0:32], sA[:, 0:32], ALU.mult))
                    mk.v(lambda v: v.reduce_sum(d12[:, 0:4], B3g_, axis=AX.X))
                    mk.v(lambda v: v.tensor_scalar(
                        d12[:, 4:8], est2_, float(CCAP), None, ALU.mult))
                    mk.v(lambda v: v.tensor_tensor(
                        d12[:, 4:8], d12[:, 4:8], d12[:, 0:4], ALU.add))
                    tdd2 = mk.tok("vS")
                    mk.dma("q", lambda q: q.dma_start(
                        dbg[:, 28 * NS - 8:28 * NS], d12[:, 0:8]),
                        waits=[tdd2], sem="dDb")
                    mk.barrier()
            g_es.close()

            e_es = ExitStack()
            esb = lambda n_, sh, dt: e_es.enter_context(
                nc.sbuf_tensor(n_ + "_e%d" % ci, sh, dt))
            wb = esb("wb", [128, NCH * D], BF16)
            wc = esb("wc", [128, 8 * D], BF16)
            uT_c = esb("uTc", [128, NCH * CCAP], BF16)
            uld = esb("uld", [128, 3 * D], BF16)
            H4 = esb("H4", [128, 8 * CT], BF16)
            ystage = esb("ystage", [128, 2 * D], F32)
            b1t_sb = esb("b1t", [128, E * 32], F32)
            tok_b1t = mk.dma("q", lambda q: q.dma_start(
                b1t_sb[:].rearrange("p (e c) -> p e c", c=32),
                b1t.rearrange("e p c -> p e c")), sem="dB1s%d" % ci)

            def w1_load(e, fcB, waits):
                return mk.dma("p", lambda g: g.dma_start(
                    wb[:].rearrange("p (c f) -> p c f", f=D),
                    w1b_d[e, :, fcB * D:(fcB + 1) * D].rearrange(
                        "(c p) f -> p c f", p=128)),
                    waits=list(waits) + [wcast["w1"][e]],
                    sem="dW1%d" % ci)

            def w2_load(e, fcB, waits):
                return mk.dma("p", lambda g: g.dma_start(
                    wc[:].rearrange("p (c d) -> p c d", d=D),
                    w2b_d[e, fcB * D:(fcB + 1) * D, :].rearrange(
                        "(c p) d -> p c d", p=128)),
                    waits=list(waits) + [wcast["w2"][e]],
                    sem="dW2%d" % ci)

            SEQ = [(e, fb) for e in range(E) for fb in range(NFCB)]
            w1tok = w1_load(0, 0, list(sc_toks))
            w2tok = w2_load(0, 0, list(sc_toks))
            ybfree = [None] * 6
            hev = [None] * 8
            yst_tok = None
            yst2 = [None, None]
            uTc_tok = None
            te4_ev = [None] * 3
            tpb_ev = [None, None]

            NSC = CCAP // 128
            for e in range(E):
                with ExitStack() as tps:
                    tpt = [tps.enter_context(nc.psum_tensor(
                        "tpe_%d_%d_%d" % (ci, e, i), [128, NCH * 128], BF16))
                        for i in range(2)]
                    for ld in range(NSC):
                        b = ld % 3
                        ul = mk.dma("q", lambda q, e_=e, l_=ld, b_=b: q.dma_start(
                            uld[:, b_ * D:(b_ + 1) * D],
                            usort_d[e_ * CCAP + l_ * 128:
                                    e_ * CCAP + (l_ + 1) * 128, :]),
                            waits=[te4_ev[b]] + (sc_toks if e == 0 else []),
                            sem="dUl%d_%d" % (ci, b))
                        ebw = (list(ybfree) + hev[6:8]
                               if ld < 2 and e > 0 else [])
                        te4_ev[b] = tpb_ev[ld % 2] = emit_transpose8(
                            uld[:, b * D:(b + 1) * D], tpt[ld % 2],
                            uT_c[:].rearrange("p (c t) -> p c t", t=CCAP)
                            [:, :, ld * 128:(ld + 1) * 128],
                            waits_pe=[ul, te4_ev[b], tpb_ev[ld % 2]] + ebw)
                    uTc_tok = mk.tok("vS")
                with ExitStack() as yps:
                    Y = [yps.enter_context(nc.psum_tensor(
                        "Y_%d_%d_%d" % (ci, e, k), [128, 512], F32))
                        for k in range(6)]
                    Hp = [yps.enter_context(nc.psum_tensor(
                        "Hp_%d_%d_%d" % (ci, e, i), [128, CT], F32))
                        for i in range(2)]
                    for fcB in range(NFCB):
                        for tl in range(NTC):
                            for fc in range(8):
                                for c in range(NCH):
                                    mk.t(lambda t, fc_=fc, c_=c, tl_=tl: t.matmul(
                                        Hp[fc_ % 2][:],
                                        wb[:, c_ * D + fc_ * 128:
                                           c_ * D + fc_ * 128 + 128],
                                        uT_c[:, c_ * CCAP + tl_ * CT:
                                             c_ * CCAP + tl_ * CT + CT],
                                        start=(c_ == 0), stop=(c_ == NCH - 1)),
                                        waits=([w1tok, uTc_tok,
                                                hev[fc - 2] if fc >= 2 else
                                                hev[6 + fc]]
                                               if c == 0 else []),
                                        mark=(c == NCH - 1))

                                tmfc = mk.tok("tS")
                                hev[fc] = mk.s(lambda sc, fc_=fc, e_=e, fb=fcB:
                                               sc.activation(
                                    H4[:, fc_ * CT:(fc_ + 1) * CT],
                                    Hp[fc_ % 2][:], AF.Gelu,
                                    bias=b1t_sb[:, e_ * 32 + fb * 8 + fc_:
                                                e_ * 32 + fb * 8 + fc_ + 1]),
                                    waits=[tmfc, tok_b1t])
                            idx = e * NFCB + fcB
                            if tl == NTC - 1 and idx + 1 < len(SEQ):
                                ne, nf = SEQ[idx + 1]
                                w1tok = w1_load(ne, nf, [mk.tok("tS")])
                            for fc in range(8):
                                for su in range(3):
                                    for half in range(2):
                                        k = su * 2 + half
                                        wl = []
                                        if su == 0 and half == 0:
                                            wl += [w2tok, hev[fc]]
                                        if fc == 0:
                                            wl.append(ybfree[k])
                                        mk.t(lambda t, fc_=fc, su_=su, h_=half,
                                             k_=k: t.matmul(
                                            Y[k_][:],
                                            H4[:, fc_ * CT + su_ * 128:
                                               fc_ * CT + su_ * 128 + 128],
                                            wc[:, fc_ * D + h_ * 512:
                                               fc_ * D + h_ * 512 + 512],
                                            start=(fc_ == 0), stop=(fc_ == 7)),
                                            waits=wl,
                                            mark=(fc == 7 and k == 5))
                            tmy = mk.tok("tS")
                            if tl == NTC - 1 and idx + 1 < len(SEQ):
                                ne, nf = SEQ[idx + 1]
                                w2tok = w2_load(ne, nf, [tmy])
                            for su in range(3):
                                yb = su % 2
                                for half in range(2):
                                    k = su * 2 + half
                                    ybfree[k] = mk.v(
                                        lambda v, h_=half, k_=k, b_=yb:
                                        v.tensor_copy(
                                            ystage[:, b_ * D + h_ * 512:
                                                   b_ * D + h_ * 512 + 512],
                                            Y[k_][:]),
                                        waits=[tmy, yst2[yb]])
                                yst2[yb] = yst_tok = mk.dma(
                                    "p", lambda g, e_=e, tl_=tl, su_=su, b_=yb,
                                    acc=(fcB > 0): g.dma_start(
                                        ysort_f32[e_ * CCAP + tl_ * CT +
                                                  su_ * 128:
                                                  e_ * CCAP + tl_ * CT +
                                                  su_ * 128 + 128, :],
                                        ystage[:, b_ * D:(b_ + 1) * D],
                                        accum_op=(ALU.add if acc
                                                  else ALU.bypass)),
                                    waits=[ybfree[su * 2 + 1]],
                                    sem="dYs%d_%d" % (ci, yb))
            e_es.close()

            ga_es = ExitStack()
            gasb = lambda n_, sh, dt: ga_es.enter_context(
                nc.sbuf_tensor(n_ + "_a%d" % ci, sh, dt))
            g12 = gasb("g12", [128, 4 * D], F32)
            cmb8 = gasb("cmb8", [128, 32], F32)
            cmb_bf = gasb("cmb_bf", [128, 16], BF16)
            cT8 = gasb("cT8", [8, 256], BF16)
            b2s = gasb("b2s", [8, D], BF16)
            tok_b2s = mk.dma("p", lambda g: g.dma_start(b2s[:], b2),
                             sem="dB2g%d" % ci)
            with ExitStack() as gps:
                ctp8 = [gps.enter_context(nc.psum_tensor(
                    "ctp8_%d_%d" % (ci, i), [128, 128], BF16))
                    for i in range(2)]
                b2pp = [gps.enter_context(nc.psum_tensor(
                    "b2pp_%d_%d" % (ci, i), [128, 1024], F32)) for i in range(2)]
                g_tok = {}
                x_tok = [None, None]
                ct_ev = [None, None]
                cbf_tok = [None, None]
                b2x = [None, None]
                b2m = [None, None]

                def ga_stage1(s_):
                    p = s_ % 2
                    for col in (0, 1):
                        g_tok[(s_, col)] = mk.dma(
                            "p", lambda g, s__=s_, c_=col, p_=p:
                            g.indirect_dma_start(
                                g12[:, (2 * p_ + c_) * D:
                                    (2 * p_ + c_ + 1) * D], None, ysort_f32,
                                bass.IndirectOffsetOnAxis(
                                    idxq[:, 2 * s__ + c_:2 * s__ + c_ + 1], 0)),
                            waits=[yst_tok, x_tok[p]],
                            sem="dGa%d_%d" % (ci, 2 * p + col))
                    mk.v(lambda v, s__=s_, p_=p: v.tensor_scalar(
                        cmb8[:, p_ * 16:p_ * 16 + 8], iota8[:],
                        e12_sb[:, 2 * s__:2 * s__ + 1],
                        sig_sb[:, 2 * s__:2 * s__ + 1],
                        ALU.is_equal, ALU.mult), waits=[cbf_tok[p]])
                    mk.v(lambda v, s__=s_, p_=p: v.tensor_scalar(
                        cmb8[:, p_ * 16 + 8:p_ * 16 + 16], iota8[:],
                        e12_sb[:, 2 * s__ + 1:2 * s__ + 2],
                        sig_sb[:, 2 * s__ + 1:2 * s__ + 2],
                        ALU.is_equal, ALU.mult))
                    mk.v(lambda v, p_=p: v.tensor_tensor(
                        cmb8[:, p_ * 16:p_ * 16 + 8],
                        cmb8[:, p_ * 16:p_ * 16 + 8],
                        cmb8[:, p_ * 16 + 8:p_ * 16 + 16], ALU.add))
                    cbf_tok[p] = mk.v(lambda v, p_=p: v.tensor_copy(
                        cmb_bf[:, p_ * 8:(p_ + 1) * 8],
                        cmb8[:, p_ * 16:p_ * 16 + 8]))

                def ga_stage2(s_):
                    p = s_ % 2
                    t_ct = mk.t(lambda t, p_=p: t.matmul(
                        ctp8[p_][0:8, :],
                        cmb_bf[:, p_ * 8:(p_ + 1) * 8], ident[:],
                        start=True, stop=True, is_transpose=True),
                        waits=[cbf_tok[p], ct_ev[p]], mark=True)
                    ct_ev[p] = mk.s(lambda sc, p_=p: sc.copy(
                        cT8[:, p_ * 128:(p_ + 1) * 128],
                        ctp8[p_][0:8, :]), waits=[t_ct])
                    tmb2 = None
                    for half in range(2):
                        tmb2 = mk.t(lambda t, h_=half, p_=p: t.matmul(
                            b2pp[p_][:, h_ * 512:(h_ + 1) * 512],
                            cT8[:, p_ * 128:(p_ + 1) * 128],
                            b2s[0:8, h_ * 512:(h_ + 1) * 512],
                            start=True, stop=True),
                            waits=[ct_ev[p], tok_b2s, b2m[p]],
                            mark=(half == 1))
                    for col in (0, 1):
                        x_tok[p] = mk.v(lambda v, s__=s_, c_=col, p_=p:
                                        v.scalar_tensor_tensor(
                            xs(s__),
                            g12[:, (2 * p_ + c_) * D:(2 * p_ + c_ + 1) * D],
                            sig_sb[:, 2 * s__ + c_:2 * s__ + c_ + 1],
                            xs(s__), ALU.mult, ALU.add),
                            waits=[g_tok[(s_, col)]])
                    b2m[p] = x_tok[p] = mk.v(
                        lambda v, s__=s_, p_=p: v.scalar_tensor_tensor(
                            xs(s__), b2pp[p_][:], 1.0, xs(s__),
                            ALU.mult, ALU.add), waits=[tmb2])

                for s_ in range(NS):
                    ga_stage1(s_)
                    if s_ >= 1:
                        ga_stage2(s_ - 1)
                ga_stage2(NS - 1)
            ga_es.close()
            f_es.close()
            mk.barrier()

        def emit_attn_b():
            pre_es = ExitStack()
            sqh[0] = pre_es.enter_context(
                nc.sbuf_tensor("sqb", [128, D], BF16))
            st = stats_sweep([])
            ge = [g for g in g_blocks if g % 2 == 0]
            go = [g for g in g_blocks if g % 2 == 1]
            assert len(ge) == 1 and len(go) == 1, \
                "gcol path needs one even + one odd global block"
            gord = [ge[0], go[0]]
            gset = set(g_blocks)

            b0_es = ExitStack()
            b0sb = lambda n_, sh, dt: b0_es.enter_context(
                nc.sbuf_tensor(n_, sh, dt))
            u3 = b0sb("u3", [128, 2 * D], BF16)
            uT3 = b0sb("uT3", [128, NCH * 512], BF16)
            wq3 = b0sb("wq3", [128, NCH * D], BF16)
            wk3 = b0sb("wk3", [128, NCH * D], BF16)
            wv3 = b0sb("wv3", [128, NCH * D], BF16)
            qkst = b0sb("qkst", [128, NCH * 512], BF16)
            vst = b0sb("vst", [128, 2 * D], BF16)
            bq_sb = b0sb("bq_sb", [128, NCH], F32)
            bk_sb = b0sb("bk_sb", [128, NCH], F32)
            bvB = b0sb("bvB", [128, D], BF16)
            row3 = b0sb("row3", [1, D], BF16)

            def wloadT(dst, src):
                return mk.dma("p", lambda g: g.dma_start(
                    dst[:].rearrange("p (c j) -> p c j", j=D),
                    src.rearrange("(c p) j -> p c j", p=128)))

            tok_w3 = [wloadT(wq3, bWqT), wloadT(wk3, bWkT), wloadT(wv3, bWvT)]
            tok_bq = mk.dma("q", lambda q: q.dma_start(bq_sb[:], bqr),
                            sem="dBq")
            tok_bk = mk.dma("q", lambda q: q.dma_start(bk_sb[:], bkr),
                            sem="dBk")
            with ExitStack() as ps:
                pst = lambda n_, sh, dt: ps.enter_context(
                    nc.psum_tensor(n_, sh, dt))
                bc3 = pst("bc3", [128, 1024], F32)
                tpb = [pst(f"tpb{i}", [128, NCH * 128], BF16) for i in range(2)]
                qp3 = [pst(f"qp3{i}", [128, 512], F32) for i in range(2)]
                vp3 = [pst(f"vp3{i}", [128, 512], F32) for i in range(2)]

                tokr = mk.dma("p", lambda g: g.dma_start(row3[:], bv[None, :]),
                              sem="dBv")
                tm = None
                for hh in range(2):
                    tm = mk.t(lambda t, h_=hh: t.matmul(
                        bc3[:, h_ * 512:(h_ + 1) * 512], ones_bfr[:],
                        row3[:, h_ * 512:(h_ + 1) * 512], start=True, stop=True),
                        waits=[tokr], mark=(hh == 1))
                tok_bv = mk.v(lambda v: v.tensor_copy(bvB[:], bc3[:]),
                              waits=[tm])

                uT3_tok = [None, None]
                qk_evac = [None, None]
                v_evac = [None, None]
                st_dma = [None, None, None]
                for T in range(NT):
                    for su in range(4):
                        b = su % 2
                        t_u = emit_u(T * 4 + su, u3[:, b * D:(b + 1) * D],
                                     waits=[st[T * 4 + su], uT3_tok[b]])
                        uT3_tok[b] = emit_transpose8(
                            u3[:, b * D:(b + 1) * D], tpb[b],
                            uT3[:].rearrange("p (c t) -> p c t", t=512)
                            [:, :, su * 128:su * 128 + 128],
                            waits_pe=[t_u, uT3_tok[b]])
                    uT3tk = uT3_tok[1]
                    for wi, (Wsb, bias_sb, dst) in enumerate(
                            ((wq3, bq_sb, qT_d), (wk3, bk_sb, kT_d))):
                        for c8 in range(NCH):
                            bq_ = c8 % 2
                            tm = None
                            for c in range(NCH):
                                tm = mk.t(lambda t, c_=c, c8_=c8, b_=bq_,
                                          W=Wsb: t.matmul(
                                    qp3[b_][:],
                                    W[:, c_ * D + c8_ * 128:
                                      c_ * D + c8_ * 128 + 128],
                                    uT3[:, c_ * 512:(c_ + 1) * 512],
                                    start=(c_ == 0), stop=(c_ == NCH - 1)),
                                    waits=([uT3tk, tok_w3[wi], qk_evac[bq_]]
                                           if c == 0 else []),
                                    mark=(c == NCH - 1))
                            qk_evac[bq_] = mk.s(
                                lambda sc, c8_=c8, b_=bq_, bs=bias_sb:
                                sc.activation(
                                    qkst[:, c8_ * 512:(c8_ + 1) * 512],
                                    qp3[b_][:], AF.Identity,
                                    bias=bs[:, c8_:c8_ + 1], scale=1.0),
                                waits=[tm, tok_bq, tok_bk, st_dma[0],
                                       st_dma[1]])
                        st_dma[wi] = mk.dma("q", lambda q, T_=T, d_=dst: q.dma_start(
                            d_.rearrange("(c p) t -> p c t", p=128)
                            [:, :, T_ * 512:(T_ + 1) * 512],
                            qkst[:].rearrange("p (c t) -> p c t", t=512)),
                            waits=[qk_evac[0], qk_evac[1]],
                            sem="dQK%d" % wi)
                    for su in range(4):
                        b = su % 2
                        tm = None
                        for half in range(2):
                            for c in range(NCH):
                                tm = mk.t(lambda t, c_=c, h_=half, su_=su,
                                          b_=b: t.matmul(
                                    vp3[b_][:] if h_ == 0 else qp3[b_][:],
                                    uT3[:, c_ * 512 + su_ * 128:
                                        c_ * 512 + su_ * 128 + 128],
                                    wv3[:, c_ * D + h_ * 512:
                                        c_ * D + h_ * 512 + 512],
                                    start=(c_ == 0), stop=(c_ == NCH - 1)),
                                    waits=([uT3tk, tok_w3[2], v_evac[b],
                                            qk_evac[b]] if c == 0 else []),
                                    mark=(c == NCH - 1))
                        for half in range(2):
                            v_evac[b] = qk_evac[b] = mk.v(
                                lambda v, h_=half, su_=su, b_=b:
                                v.scalar_tensor_tensor(
                                    vst[:, b_ * D + h_ * 512:
                                        b_ * D + h_ * 512 + 512],
                                    (vp3[b_] if h_ == 0 else qp3[b_])[:], 1.0,
                                    bvB[:, h_ * 512:(h_ + 1) * 512],
                                    ALU.mult, ALU.add),
                                waits=[tm, tok_bv, st_dma[2]])
                        st_dma[2] = mk.dma("q", lambda q, T_=T, su_=su, b_=b:
                                           q.dma_start(
                            v_d[(T_ * 4 + su_) * 128:(T_ * 4 + su_ + 1) * 128, :],
                            vst[:, b_ * D:(b_ + 1) * D]),
                            waits=[v_evac[b]], sem="dV3")
            mk.barrier()
            b0_es.close()
            if STAGE_B == "b0":
                pre_es.close()
                return

            b1_es = ExitStack()
            b1sb = lambda n_, sh, dt: b1_es.enter_context(
                nc.sbuf_tensor(n_, sh, dt))
            NW = NB + 2
            TW = NS + 1
            kTw2 = b1sb("kTw2", [128, NW * 64], BF16)
            vwA = b1sb("vwA", [128, TW * 64], BF16)
            vwB = b1sb("vwB", [128, TW * 64], BF16)
            vplA = b1sb("vplA", [128, n_tok // 2], BF16)
            vplB = b1sb("vplB", [128, n_tok // 2], BF16)
            qT2 = b1sb("qT2", [128, n_tok], BF16)
            ctx2 = b1sb("ctx2", [128, n_tok], BF16)
            kg2 = b1sb("kg2", [128, 128], BF16)
            vg2A = b1sb("vg2A", [128, 64], BF16)
            vg2B = b1sb("vg2B", [128, 64], BF16)
            Plc = b1sb("Plc", [128, 1536], BF16)
            sls = b1sb("sls", [128, 24], F32)
            gsls = b1sb("gsls", [128, 16], F32)
            t128 = b1sb("t128", [128, 128], F32)
            Pgr = b1sb("Pgr", [64, n_tok], BF16)
            PTg_sb = b1sb("PTg_sb", [128, n_tok // 2], BF16)


            NA = NB // 2
            prev_pair_pe = None
            prev_pair_dve = []
            for pr in range(H // 2):
                hA, hB = 2 * pr, 2 * pr + 1
                ltoks_d = {}
                class _LT:
                    def append(self, tok):
                        ltoks_d[tok[0]] = tok
                ltoks = _LT()
                for hi, hh in enumerate((hA, hB)):
                    pb = hi * 64
                    ltoks.append(mk.dma("q", lambda q, h_=hh, p_=pb: q.dma_start(
                        qT2[p_:p_ + 64, :], qT_d[h_ * 64:(h_ + 1) * 64, :]),
                        waits=[prev_pair_pe], sem="dP1_%d" % hi))
                    ltoks.append(mk.dma("q", lambda q, h_=hh, p_=pb: q.dma_start(
                        kTw2[p_:p_ + 64, 64:64 + n_tok],
                        kT_d[h_ * 64:(h_ + 1) * 64, :]),
                        waits=[prev_pair_pe], sem="dP2_%d" % hi))
                    ltoks.append(mk.dma("q", lambda q, h_=hh, p_=pb: q.dma_start(
                        kTw2[p_:p_ + 64, 0:64],
                        kT_d[h_ * 64:(h_ + 1) * 64, n_tok - 64:n_tok]),
                        waits=[prev_pair_pe], sem="dP3_%d" % hi))
                    ltoks.append(mk.dma("q", lambda q, h_=hh, p_=pb: q.dma_start(
                        kTw2[p_:p_ + 64, 64 + n_tok:128 + n_tok],
                        kT_d[h_ * 64:(h_ + 1) * 64, 0:64]),
                        waits=[prev_pair_pe], sem="dP4_%d" % hi))
                    vw = (vwA, vwB)[hi]
                    hc = slice(hh * 64, (hh + 1) * 64)
                    ltoks.append(mk.dma("q", lambda q, h_=hh, v_=vw: q.dma_start(
                        v_[:].rearrange("p (j d) -> p j d", d=64)[:, 1:NS, :],
                        v_d[64:n_tok - 64, h_ * 64:(h_ + 1) * 64].rearrange(
                            "(j p) d -> p j d", p=128)),
                        waits=[prev_pair_pe] + prev_pair_dve,
                        sem="dP5_%d" % hi))
                    for (dp, sr) in ((0, n_tok - 64), (64, 0)):
                        for jt in (0, NS):
                            ltoks.append(mk.dma(
                                "q", lambda q, h_=hh, v_=vw, d_=dp, s_=sr,
                                j_=jt: q.dma_start(
                                    v_[d_:d_ + 64, j_ * 64:(j_ + 1) * 64],
                                    v_d[s_:s_ + 64, h_ * 64:(h_ + 1) * 64]),
                                waits=[prev_pair_pe] + prev_pair_dve,
                                sem="dP6_%d" % hi))
                    vp = (vplA, vplB)[hi]
                    ltoks.append(mk.dma("q", lambda q, h_=hh, v_=vp: q.dma_start(
                        v_[:].rearrange("p (j d) -> p j d", d=64),
                        v_d[:, h_ * 64:(h_ + 1) * 64].rearrange(
                            "(j p) d -> p j d", p=128)),
                        waits=[prev_pair_pe] + prev_pair_dve,
                        sem="dP7_%d" % hi))
                ltoks = list(ltoks_d.values())
                kgtok = None
                for gi, gb in enumerate(gord):
                    kgtok = mk.v(lambda v, g_=gb, i_=gi: v.tensor_copy(
                        kg2[:, i_ * 64:(i_ + 1) * 64],
                        kTw2[:, 64 + g_ * 64:64 + (g_ + 1) * 64]),
                        waits=ltoks + [prev_pair_pe] + prev_pair_dve)
                for vg2, vp_ in ((vg2A, vplA), (vg2B, vplB)):
                    mk.v(lambda v, d_=vg2, s_=vp_, g_=gord[0]: v.tensor_copy(
                        d_[0:64, :], s_[0:64, (g_ // 2) * 64:(g_ // 2) * 64 + 64]))
                    kgtok = mk.v(lambda v, d_=vg2, s_=vp_, g_=gord[1]:
                                 v.tensor_copy(
                        d_[64:128, :],
                        s_[64:128, (g_ // 2) * 64:(g_ // 2) * 64 + 64]))

                with ExitStack() as ps:
                    pst = lambda n_, sh, dt: ps.enter_context(
                        nc.psum_tensor(n_, sh, dt))
                    Sp = [pst(f"Sp{pr}_{i}", [128, 384], F32) for i in range(3)]
                    PTp = [pst(f"PTp{pr}_{i}", [128, 384], BF16) for i in range(2)]
                    OpA = [pst(f"OpA{pr}_{i}", [128, 64], F32) for i in range(2)]

                    s_free = [None, None, None]
                    pt_free = [None, None]
                    oA_free = [None, None]
                    av_tok = [None] * 4
                    ctx_tok = [None] * 4

                    def stage1(it):
                        hi, a, r = it["hi"], it["a"], it["r"]
                        sn = it["sn"]
                        bh = hi * 64
                        sl = sls[:, sn * 6:(sn + 1) * 6]
                        mk.t(lambda t, a_=a, b_=bh, r_=r: t.matmul(
                            Sp[r_][:, 0:256],
                            qT2[b_:b_ + 64, a_ * 128:a_ * 128 + 128],
                            kTw2[b_:b_ + 64, a_ * 128:a_ * 128 + 256],
                            start=True, stop=False),
                            waits=[s_free[r], kgtok] + ltoks)
                        it["tsc"] = mk.t(lambda t, a_=a, b_=bh, r_=r: t.matmul(
                            Sp[r_][:, 256:384],
                            qT2[b_:b_ + 64, a_ * 128:a_ * 128 + 128],
                            kg2[b_:b_ + 64, :],
                            start=False, stop=True), mark=True)
                        mk.v(lambda v, r_=r, sl_=sl: v.reduce_max(
                            sl_[:, 0:1], Sp[r_][:, 0:256], axis=AX.X),
                            waits=[it["tsc"]], ns=True)
                        mk.v(lambda v, r_=r, sl_=sl: v.reduce_max(
                            sl_[:, 1:2], Sp[r_][:, 256:384], axis=AX.X),
                            ns=True)
                        mk.v(lambda v, r_=r: v.memset(
                            Sp[r_][0:64, 192:256], -1e9))
                        mk.v(lambda v, r_=r: v.memset(
                            Sp[r_][64:128, 0:64], -1e9), ns=True)
                        mk.v(lambda v, sl_=sl: v.tensor_scalar(
                            sl_[:, 0:2], sl_[:, 0:2], -0.125, None, ALU.mult),
                            ns=True)
                        it["tmax"] = mk.tok("vS")

                    def s2a(it):
                        r, sn = it["r"], it["sn"]
                        P_ = Plc[:, sn * 384:(sn + 1) * 384]
                        sl = sls[:, sn * 6:(sn + 1) * 6]
                        mk.s(lambda sc, r_=r, P=P_, sl_=sl: sc.activation(
                            P[:, 0:256], Sp[r_][:, 0:256], AF.Exp,
                            bias=sl_[:, 0:1], scale=0.125,
                            accum_out=sl_[:, 2:3]),
                            waits=[it["tmax"], av_tok[sn], ctx_tok[sn]],
                            ns=True)
                        s_free[r] = t_e2 = mk.s(
                            lambda sc, r_=r, P=P_, sl_=sl: sc.activation(
                                P[:, 256:384], Sp[r_][:, 256:384], AF.Exp,
                                bias=sl_[:, 1:2], scale=0.125,
                                accum_out=sl_[:, 3:4]), ns=True)
                        mk.v(lambda v, sl_=sl: v.reciprocal(
                            sl_[:, 3:4], sl_[:, 3:4]),
                            waits=[t_e2, ctx_tok[sn]], ns=True)
                        mk.v(lambda v, sl_=sl: v.tensor_tensor(
                            sl_[:, 4:5], sl_[:, 2:3], sl_[:, 3:4], ALU.mult))
                        mk.v(lambda v, sl_=sl: v.reciprocal(
                            sl_[:, 2:3], sl_[:, 2:3]))
                        it["t_pn"] = mk.v(
                            lambda v, P=P_, sl_=sl: v.tensor_scalar(
                                P[:, 256:384], P[:, 256:384], sl_[:, 4:5],
                                None, ALU.mult), ns=True)

                    def s2b(it):
                        r2, sn = it["r2"], it["sn"]
                        P_ = Plc[:, sn * 384:(sn + 1) * 384]
                        tm = None
                        for c in range(3):
                            tm = mk.t(lambda t, c_=c, r_=r2, P=P_: t.matmul(
                                PTp[r_][:, c_ * 128:(c_ + 1) * 128],
                                P[:, c_ * 128:(c_ + 1) * 128], ident[:],
                                start=(c_ == 0), stop=(c_ == 2),
                                is_transpose=True),
                                waits=([it["t_pn"], pt_free[r2]]
                                       if c == 0 else []),
                                mark=(c == 2))
                        it["ptev"] = pt_free[r2] = mk.s(
                            lambda sc, r_=r2, P=P_: sc.copy(
                                P[:, 0:384], PTp[r_][:, 0:384]), waits=[tm],
                            ns=True)

                    def s2c(it):
                        hi, a, r2, sn = it["hi"], it["a"], it["r2"], it["sn"]
                        P_ = Plc[:, sn * 384:(sn + 1) * 384]
                        sl = sls[:, sn * 6:(sn + 1) * 6]
                        vw = (vwA, vwB)[hi]
                        for c in range(2):
                            mk.t(lambda t, c_=c, a_=a, v_=vw, r_=r2, P=P_:
                                 t.matmul(
                                OpA[r_][:],
                                P[:, c_ * 128:(c_ + 1) * 128],
                                v_[:, (a_ + c_) * 64:(a_ + c_) * 64 + 64],
                                start=(c_ == 0), stop=False),
                                waits=([it["ptev"], oA_free[r2]]
                                       if c == 0 else []))
                        vg2 = (vg2A, vg2B)[hi]
                        tO = av_tok[sn] = mk.t(lambda t, v_=vg2, r_=r2, P=P_:
                                               t.matmul(
                            OpA[r_][:], P[:, 256:384], v_[:],
                            start=False, stop=True), mark=True)
                        oA_free[r2] = ctx_tok[sn] = mk.v(
                            lambda v, r_=r2, a_=a, h_=hi, sl_=sl:
                            v.tensor_scalar(
                                ctx2[:, h_ * (n_tok // 2) + a_ * 64:
                                     h_ * (n_tok // 2) + a_ * 64 + 64],
                                OpA[r_][:], sl_[:, 2:3], None, ALU.mult),
                            waits=[tO])

                    iters = []
                    rot = 0
                    for a in range(NA):
                        if 2 * a in gset and 2 * a + 1 in gset:
                            continue
                        for hi in range(2):
                            iters.append({"hi": hi, "a": a, "r": rot % 3,
                                          "r2": rot % 2, "sn": rot % 4})
                            rot += 1
                    n_it = len(iters)
                    for i in range(n_it + 4):
                        if i < n_it:
                            stage1(iters[i])
                        if 0 <= i - 2 < n_it:
                            s2a(iters[i - 2])
                        if 0 <= i - 3 < n_it:
                            s2b(iters[i - 3])
                        if 0 <= i - 4 < n_it:
                            s2c(iters[i - 4])
                    lc_pe = mk.tok("tS")
                    lc_dve = mk.tok("vS")

                with ExitStack() as ps:
                    pst = lambda n_, sh, dt: ps.enter_context(
                        nc.psum_tensor(n_, sh, dt))
                    Sgr = [pst(f"Sgr{pr}_{i}", [64, 512], F32) for i in range(2)]
                    PTgp = [pst(f"PTgp{pr}_{i}", [128, 512], BF16) for i in range(2)]
                    Ogr = pst(f"Ogr{pr}", [64, 64], F32)

                    prev_ctx = None
                    for hi, hh in enumerate((hA, hB)):
                        bh = hi * 64
                        vp = (vplA, vplB)[hi]
                        for gb in g_blocks:
                            sev = [[], []]
                            pst_toks = []
                            for kc in range(NT):
                                r = kc % 2
                                tm = mk.t(lambda t, b_=bh, g_=gb, r_=r,
                                          k_=kc: t.matmul(
                                    Sgr[r_][:],
                                    qT2[b_:b_ + 64, g_ * 64:(g_ + 1) * 64],
                                    kTw2[b_:b_ + 64, 64 + k_ * 512:
                                         64 + (k_ + 1) * 512],
                                    start=True, stop=True),
                                    waits=[lc_pe, lc_dve] + sev[r],
                                    mark=True)
                                pst_toks.append(mk.v(
                                    lambda v, r_=r, k_=kc: v.tensor_scalar(
                                    Pgr[0:64, k_ * 512:(k_ + 1) * 512],
                                    Sgr[r_][0:64, :], 0.125, None,
                                    ALU.mult), waits=[tm]))
                                sev[r] = [pst_toks[-1], mk.v(
                                    lambda v, r_=r, k_=kc: v.reduce_max(
                                    gsls[0:64, k_:k_ + 1],
                                    Sgr[r_][0:64, :], axis=AX.X), waits=[tm])]
                            mk.v(lambda v: v.reduce_max(
                                sls[0:64, 0:1], gsls[0:64, 0:NT], axis=AX.X))
                            mx_tok = mk.v(lambda v: v.tensor_scalar(
                                sls[0:64, 0:1], sls[0:64, 0:1], -0.125,
                                None, ALU.mult))
                            ev2 = [None, None]
                            ex_t = [None] * (NS // 4)
                            ptg_ev = [None] * (NS // 4)
                            tmav = None
                            for bk in range(NS // 4):
                                ex_t[bk] = mk.s(
                                    lambda sc, b_=bk, p_=pst_toks[
                                        2 * bk:2 * bk + 2]: sc.activation(
                                    Pgr[0:64, b_ * 512:(b_ + 1) * 512],
                                    Pgr[0:64, b_ * 512:(b_ + 1) * 512],
                                    AF.Exp, bias=sls[0:64, 0:1], scale=1.0,
                                    accum_out=gsls[0:64, NT + b_:
                                                   NT + b_ + 1]),
                                    waits=[mx_tok] + pst_toks)
                                tm = None
                                for c in range(4):
                                    kc = bk * 4 + c
                                    tm = mk.t(lambda t, c_=c, k_=kc,
                                              r_=bk % 2: t.matmul(
                                        PTgp[r_][:, c_ * 128:c_ * 128 + 64],
                                        Pgr[0:64, k_ * 128:(k_ + 1) * 128],
                                        ident[0:64, 0:64],
                                        start=(c_ == 0), stop=(c_ == 3),
                                        is_transpose=True),
                                        waits=([ex_t[bk], ev2[bk % 2]]
                                               if c == 0 else []),
                                        mark=(c == 3))
                                ev2[bk % 2] = ptg_ev[bk] = mk.v(
                                    lambda v, b_=bk, r_=bk % 2:
                                    v.tensor_copy(
                                    PTg_sb[:].rearrange(
                                        "p (k d) -> p k d", d=64)
                                    [:, b_ * 4:(b_ + 1) * 4, :],
                                    PTgp[r_][:].rearrange(
                                        "p (k d) -> p k d", d=128)
                                    [:, :, 0:64]), waits=[tm])
                            for kc in range(NS):
                                w = [prev_ctx] if kc == 0 else []
                                if kc % 4 == 0:
                                    w = w + [ptg_ev[kc // 4]]
                                tmav = mk.t(lambda t, k_=kc, v_=vp,
                                            lst=(kc == NS - 1): t.matmul(
                                    Ogr[:],
                                    PTg_sb[:, k_ * 64:(k_ + 1) * 64],
                                    v_[:, k_ * 64:(k_ + 1) * 64],
                                    start=(k_ == 0), stop=lst),
                                    waits=w, mark=(kc == NS - 1))
                            mk.v(lambda v: v.reduce_sum(
                                sls[0:64, 1:2], gsls[0:64, NT:NT + NS // 4],
                                axis=AX.X), waits=[ex_t[NS // 4 - 1]])
                            t_rc = mk.v(lambda v: v.reciprocal(
                                sls[0:64, 1:2], sls[0:64, 1:2]))
                            prev_ctx = mk.s(
                                lambda sc, g_=gb, h_=hi: sc.activation(
                                ctx2[(g_ % 2) * 64:(g_ % 2) * 64 + 64,
                                     h_ * (n_tok // 2) + (g_ // 2) * 64:
                                     h_ * (n_tok // 2) + (g_ // 2) * 64 + 64],
                                Ogr[:], AF.Identity, bias=0.0,
                                scale=sls[0:64, 1:2]),
                                waits=[tmav, t_rc])
                    grow_pe = mk.tok("tS")

                ctoks = []
                for hi, hh in enumerate((hA, hB)):
                    ctoks.append(mk.dma("q", lambda q, h_=hh, i_=hi: q.dma_start(
                        ctx_d[:, h_ * 64:(h_ + 1) * 64].rearrange(
                            "(j p) d -> p j d", p=128),
                        ctx2[:, i_ * (n_tok // 2):(i_ + 1) * (n_tok // 2)]
                        .rearrange("p (j d) -> p j d", d=64)),
                        waits=[mk.tok("vS"), mk.tok("sS")],
                        sem="dCtx%d" % hi))
                prev_pair_pe = grow_pe
                prev_pair_dve = ctoks
            mk.barrier()
            b1_es.close()
            if STAGE_B == "b1":
                pre_es.close()
                return

            b2_es = ExitStack()
            b2sb = lambda n_, sh, dt: b2_es.enter_context(
                nc.sbuf_tensor(n_, sh, dt))
            wo3 = b2sb("wo3", [128, NCH * D], BF16)
            cst = b2sb("cst", [128, 2 * D], BF16)
            cT3 = b2sb("cT3", [128, D], BF16)
            boB = b2sb("boB", [128, D], F32)
            row4 = b2sb("row4", [1, D], F32)
            tok_wo = wloadT(wo3, bWoT)
            with ExitStack() as ps:
                pst = lambda n_, sh, dt: ps.enter_context(
                    nc.psum_tensor(n_, sh, dt))
                bc4 = pst("bc4", [128, 1024], F32)
                tpc = pst("tpc", [128, NCH * 128], BF16)
                op3 = [pst(f"op3{i}", [128, 512], F32) for i in range(2)]

                tokr = mk.dma("q", lambda q: q.dma_start(row4[:], bo[None, :]),
                              sem="dBo")
                tm = None
                for hh in range(2):
                    tm = mk.t(lambda t, h_=hh: t.matmul(
                        bc4[:, h_ * 512:(h_ + 1) * 512], ones_f32r[:],
                        row4[:, h_ * 512:(h_ + 1) * 512], start=True, stop=True),
                        waits=[tokr], mark=(hh == 1))
                tok_bo = mk.v(lambda v: v.tensor_copy(boB[:], bc4[:]),
                              waits=[tm])

                cld = [None, None]
                ctT_ev = None
                ope3 = [None, None]
                for s_ in range(NS):
                    b = s_ % 2
                    cld[b] = mk.dma("q", lambda q, s__=s_, b_=b: q.dma_start(
                        cst[:, b_ * D:(b_ + 1) * D],
                        ctx_d[s__ * 128:(s__ + 1) * 128, :]),
                        waits=[ctT_ev if s_ >= 2 else None], sem="dC3_%d" % b)
                    ctT_ev = emit_transpose8(
                        cst[:, b * D:(b + 1) * D], tpc,
                        cT3[:].rearrange("p (c t) -> p c t", t=128),
                        waits_pe=[cld[b], ctT_ev],
                        waits_dve=[ope3[1]])
                    tm = None
                    for half in range(2):
                        for c in range(NCH):
                            tm = mk.t(lambda t, c_=c, h_=half: t.matmul(
                                op3[h_][:],
                                cT3[:, c_ * 128:(c_ + 1) * 128],
                                wo3[:, c_ * D + h_ * 512:
                                    c_ * D + h_ * 512 + 512],
                                start=(c_ == 0), stop=(c_ == NCH - 1)),
                                waits=([ctT_ev, tok_wo, ope3[half]]
                                       if c == 0 else []),
                                mark=(c == NCH - 1 and half == 1))
                    for half in range(2):
                        mk.v(lambda v, s__=s_, h_=half: v.scalar_tensor_tensor(
                            xs(s__)[:, h_ * 512:(h_ + 1) * 512],
                            op3[h_][:], 1.0,
                            xs(s__)[:, h_ * 512:(h_ + 1) * 512],
                            ALU.mult, ALU.add), waits=[tm, tok_bo])
                    ope3[0] = ope3[1] = mk.v(lambda v, s__=s_: v.tensor_tensor(
                        xs(s__), xs(s__), boB[:], ALU.add))
            mk.barrier()
            b2_es.close()
            pre_es.close()

        do_ffn = emit_ffn_sparse if SPARSE_FFN else emit_ffn
        if EMIT_ONLY == "b":
            emit_attn_b()
        else:
            emit_attn_a()
            if phases >= 2:
                do_ffn()
            if phases >= 3:
                emit_attn_b()
            if phases >= 4:
                do_ffn()

        tok_st = mk.dma("q", lambda q: q.dma_start(
            out.rearrange("(s p) d -> p s d", p=128),
            x_sb[:].rearrange("p (s d) -> p s d", d=D)))
        mk.wait("q", "dQ", mk.cnt["dQ"])

        with nc.Block() as block:
            mk.replay(block)
    return nc


_NC_CACHE = {}


def _get_nc(n_tok, g_blocks, phases=PHASES):
    key = (n_tok, tuple(g_blocks), phases)
    if key not in _NC_CACHE:
        _NC_CACHE[key] = build(n_tok, g_blocks, phases)
    return _NC_CACHE[key]


def kernel(**inputs):
    x = np.ascontiguousarray(inputs["L0_attn_a_inputs"])
    B, n_tok, _ = x.shape
    g = [int(v) for v in np.asarray(inputs["L0_attn_b_global_indices"])]
    nc = _get_nc(n_tok, g)

    b1 = np.ascontiguousarray(inputs["L0_ffn_b1"])
    b1t = np.ascontiguousarray(
        b1.reshape(E, FF // 128, 128).transpose(0, 2, 1))
    common = {
        "aWQ": np.ascontiguousarray(inputs["L0_attn_a_W_Q"]),
        "aWK": np.ascontiguousarray(inputs["L0_attn_a_W_K"]),
        "aWV": np.ascontiguousarray(inputs["L0_attn_a_W_V"]),
        "aWfc": np.ascontiguousarray(inputs["L0_attn_a_W_fc"]),
        "gamma": np.ascontiguousarray(inputs["L0_attn_a_gamma"]),
        "beta": np.ascontiguousarray(inputs["L0_attn_a_beta"]),
        "bWqT": np.ascontiguousarray(inputs["L0_attn_b_Wq"].T),
        "bWkT": np.ascontiguousarray(inputs["L0_attn_b_Wk"].T),
        "bWvT": np.ascontiguousarray(inputs["L0_attn_b_Wv"].T),
        "bWoT": np.ascontiguousarray(inputs["L0_attn_b_Wo"].T),
        "bqr": np.ascontiguousarray(
            inputs["L0_attn_b_bq"].reshape(NCH, 128).T),
        "bkr": np.ascontiguousarray(
            inputs["L0_attn_b_bk"].reshape(NCH, 128).T),
        "bv": np.ascontiguousarray(inputs["L0_attn_b_bv"]),
        "bo": np.ascontiguousarray(inputs["L0_attn_b_bo"]),
        "gate_w": np.ascontiguousarray(inputs["L0_ffn_gate_w"]),
        "w1": np.ascontiguousarray(inputs["L0_ffn_w1"]),
        "b1t": b1t,
        "w2": np.ascontiguousarray(inputs["L0_ffn_w2"]),
        "b2": np.ascontiguousarray(inputs["L0_ffn_b2"]),
    }
    in_maps = [dict(common, x=np.ascontiguousarray(x[b])) for b in range(B)]
    res = run_bass_kernel_spmd(nc, in_maps, core_ids=list(range(B)))
    return np.stack([r["out"] for r in res.results], axis=0)
